# version 27
# baseline (speedup 1.0000x reference)
import numpy as np
import ml_dtypes
from contextlib import ExitStack

import concourse.bass as bass
import concourse.mybir as mybir
from concourse.bass_utils import run_bass_kernel_spmd

F32 = mybir.dt.float32
BF16 = mybir.dt.bfloat16
I32 = mybir.dt.int32
AF = mybir.ActivationFunctionType
ALU = mybir.AluOpType
AX = mybir.AxisListType

D = 4096
T = 2048
NKT = 32
NCH = 16
CW = 256
NTT = 16
EPS = 1e-5
LNOPS = 10


def build_nc(debug=False):
    nc = bass.Bass("TRN2", target_bir_lowering=False)

    x_h = nc.dram_tensor("x", [T, D], F32, kind="ExternalInput").ap()
    wt_h = nc.dram_tensor("wt", [D, D], F32, kind="ExternalInput").ap()
    wgt_h = nc.dram_tensor("wgt", [D, D], F32, kind="ExternalInput").ap()
    mask_h = nc.dram_tensor("maskt", [D, D], I32, kind="ExternalInput").ap()
    at_h = nc.dram_tensor("at", [D, 128], F32, kind="ExternalInput").ap()
    bt_h = nc.dram_tensor("bt", [128, D], F32, kind="ExternalInput").ap()
    bias_h = nc.dram_tensor("bias", [D], F32, kind="ExternalInput").ap()
    lw_h = nc.dram_tensor("lwrep", [128, 1], F32, kind="ExternalInput").ap()
    id_h = nc.dram_tensor("ident", [128, 128], BF16, kind="ExternalInput").ap()
    g32_h = nc.dram_tensor("g32", [128, 4], BF16, kind="ExternalInput").ap()
    rep4_h = nc.dram_tensor("rep4", [4, 128], BF16, kind="ExternalInput").ap()
    ones_h = nc.dram_tensor("onesrow", [1, 128], BF16, kind="ExternalInput").ap()
    out_h = nc.dram_tensor("out", [T, D], F32, kind="ExternalOutput").ap()
    if debug:
        dbg_xnT = nc.dram_tensor("dbg_xnT", [128, 2048], F32, kind="ExternalOutput").ap()
        dbg_emT = nc.dram_tensor("dbg_emT", [128, 256], F32, kind="ExternalOutput").ap()
        dbg_h = nc.dram_tensor("dbg_h", [128, T], F32, kind="ExternalOutput").ap()
        dbg_ws = nc.dram_tensor("dbg_ws", [128, NKT * CW], F32, kind="ExternalOutput").ap()
        dbg_cv = nc.dram_tensor("dbg_cv", [128, 1], F32, kind="ExternalOutput").ap()

    A = nc.alloc_sbuf_tensor_at
    off = nc.sbuf_base

    def at_(name, shape, dtype, nbytes=None, align=32):
        nonlocal off
        off = (off + align - 1) // align * align
        t = A(name, shape, dtype, offset=off)
        sz = nbytes
        if sz is None:
            n = 1
            for s in shape[1:]:
                n *= s
            sz = n * mybir.dt.size(dtype)
        off += sz
        return t

    xnT = at_("xnT", [128, NKT * T], BF16)
    h_sb = at_("h_sb", [128, T], BF16)
    off = (off + 31) // 32 * 32
    _mask_off = off
    at_("maskraw", [128, 8 * CW], I32)
    mask_db = [A("maskdb%d" % i, [128, 2 * CW], I32, offset=_mask_off + i * 2048)
               for i in range(4)]
    mask_dbp = [A("maskdbp%d" % i, [128, 4 * CW], I32, offset=_mask_off + i * 4096)
                for i in range(2)]
    i128 = at_("i128", [128, 128], BF16)
    g32 = at_("g32s", [128, 4], BF16)
    rep4 = at_("rep4s", [4, 128], BF16, nbytes=256)
    ones_sb = at_("ones_sb", [1, 128], BF16, nbytes=256)
    lw_sb = at_("lw_sb", [128, 1], F32)
    cvec = at_("cvec", [128, 1], F32)
    s1 = [at_("s1%d" % i, [128, 1], F32) for i in range(2)]
    s2 = [at_("s2%d" % i, [128, 1], F32) for i in range(2)]
    mean = [at_("mean%d" % i, [128, 1], F32) for i in range(2)]
    vareps = [at_("vareps%d" % i, [128, 1], F32) for i in range(2)]
    m2 = [at_("m2%d" % i, [128, 1], F32) for i in range(2)]
    veps = [at_("veps%d" % i, [128, 1], F32) for i in range(2)]
    rr = [at_("rr%d" % i, [128, 1], F32) for i in range(2)]
    r0 = [at_("r0%d" % i, [128, 1], F32) for i in range(2)]
    t1 = [at_("t1%d" % i, [128, 1], F32) for i in range(2)]
    t3 = [at_("t3%d" % i, [128, 1], F32) for i in range(2)]
    rstd = [at_("rstd%d" % i, [128, 1], F32) for i in range(2)]
    nmr = [at_("nmr%d" % i, [128, 1], F32) for i in range(2)]
    ln_idx = {}
    _nd0 = 0
    for _t in range(NTT):
        for _nm in ("mean", "vareps", "m2", "veps", "rr",
                    "t1", "t3", "rstd", "nmr", "act"):
            _nd0 += 1
            ln_idx[(_t, _nm)] = _nd0
    assert _nd0 == LNOPS * NTT
    colsum = [at_("colsum%d" % i, [128, 256], BF16) for i in range(2)]
    bm = [at_("bm%d" % i, [4, 256], BF16, nbytes=512) for i in range(2)]
    emT = [at_("emT%d" % i, [128, 256], BF16) for i in range(2)]
    cbt = [at_("cbt%d" % i, [128, CW], BF16) for i in range(2)]
    bbf = [at_("bbf%d" % i, [1, CW], BF16, nbytes=512) for i in range(2)]

    arena = (off + 255) // 256 * 256
    wsum = [A("wsum%d" % i, [128, NKT * CW], BF16, offset=arena + i * 16384) for i in range(2)]
    wgt_db = [A("wgtdb%d" % i, [128, 8 * CW], BF16, offset=arena + 32768 + i * 4096) for i in range(2)]
    resid_db = [A("resid%d" % i, [128, 4 * CW], F32, offset=arena + 40960 + i * 4096) for i in range(2)]
    out_db = [A("outdb%d" % i, [128, 4 * CW], F32, offset=arena + 49152 + i * 4096) for i in range(2)]
    main_end = arena + 57344
    x_bufs = [A("x_buf%d" % i, [128, D], BF16, offset=arena + 16384 + i * 8192)
              for i in range(3)]

    def xb(t):
        return t % 3 if t < 14 else t - 14
    at_bf = A("at_bf", [128, NKT * 128], BF16, offset=arena + 40960)
    sq_scr = A("sq_scr", [128, D], BF16, offset=arena + 49152)
    assert main_end <= nc.sbuf_top, (main_end, nc.sbuf_top)

    ps_o = [nc.alloc_psum_tensor("ps_o%d" % i, [128, 512], F32) for i in range(2)]
    ps_x = [nc.alloc_psum_tensor("ps_x%d" % i, [128, 1024], BF16) for i in range(2)]
    ps_h = [nc.alloc_psum_tensor("ps_h%d" % i, [128, 512], F32) for i in range(2)]
    ps_m = nc.alloc_psum_tensor("ps_m", [128, 512], F32)
    ps_e = nc.alloc_psum_tensor("ps_e", [128, 512], F32)

    ctx = ExitStack()

    def sem(name):
        return ctx.enter_context(nc.semaphore(name))

    s_cst = sem("s_cst")
    s_x = [sem("s_x0"), sem("s_x1"), sem("s_x2")]
    s_msk = [sem("s_msk%d" % i) for i in range(4)]
    s_mskp = [sem("s_mskp%d" % i) for i in range(2)]
    s_mred2 = sem("s_mred2")
    s_rsd = [sem("s_rsd0"), sem("s_rsd1")]
    s_at = sem("s_at")
    s_wgt = [sem("s_wgt0"), sem("s_wgt1")]
    s_wW = [[sem("s_wW%d_%d" % (i, g)) for g in range(4)] for i in range(2)]
    s_cb = [sem("s_cb0"), sem("s_cb1")]
    s_bb = [sem("s_bb0"), sem("s_bb1")]
    s_mred = sem("s_mred")
    s_pem = sem("s_pem")
    s_bm = sem("s_bm")
    s_emT = sem("s_emT")
    s_sq = sem("s_sq")
    s_id = sem("s_id")
    s_dvc = sem("s_dvc")
    s_r0 = sem("s_r0")

    s_peg = sem("s_peg")
    s_sev = sem("s_sev")
    s_peh = sem("s_peh")
    s_hrdy = sem("s_hrdy")
    s_cv = sem("s_cv")
    s_wm = sem("s_wm")
    s_add = sem("s_add")
    s_pmain = sem("s_pmain")
    s_ev = sem("s_ev")
    s_out = [sem("s_out0"), sem("s_out1")]
    s_dbg = sem("s_dbg") if debug else None

    def mask_dma(eng, c, qs=tuple(range(16))):
        for q in qs:
            m = c * 16 + q
            if m >= 4:
                eng.wait_ge(s_mred, m - 3)
            eng.dma_start(
                out=mask_db[m % 4][:].rearrange("p (a b) -> p a b", a=2, b=CW),
                in_=mask_h[:, c * CW:(c + 1) * CW]
                    .rearrange("(kt p) n -> p kt n", p=128)[:, q * 2:(q + 1) * 2, :],
            ).then_inc(s_msk[m % 4], 16)

    def mask_dma_main(eng, c, qs=tuple(range(8))):
        for q in qs:
            m8 = 8 * (c - 2) + q
            if m8 == 0:
                eng.wait_ge(s_mred, 30)
            elif m8 == 1:
                eng.wait_ge(s_mred, 32)
            else:
                eng.wait_ge(s_mred2, m8 - 1)
            eng.dma_start(
                out=mask_dbp[m8 % 2][:].rearrange("p (a b) -> p a b", a=4, b=CW),
                in_=mask_h[:, c * CW:(c + 1) * CW]
                    .rearrange("(kt p) n -> p kt n", p=128)[:, q * 4:(q + 1) * 4, :],
            ).then_inc(s_mskp[m8 % 2], 16)

    def mask_reduce(vector, c, q):
        m = c * 16 + q
        par = c % 2
        vector.wait_ge(s_msk[m % 4], 16 * (m // 4 + 1))
        with nc.allow_low_precision(reason="int mask sums <=32 exact in bf16"):
            vector.tensor_reduce(
                out=colsum[par][:, q * 16:(q + 1) * 16].rearrange("p (a b) -> p a b", a=2, b=8),
                in_=mask_db[m % 4][:].rearrange("p (a b c) -> p a b c", a=2, b=8, c=32),
                axis=AX.X, op=ALU.add,
            ).then_inc(s_mred, 1)

    def mask_reduce_main(vector, c, q):
        m8 = 8 * (c - 2) + q
        par = c % 2
        vector.wait_ge(s_mskp[m8 % 2], 16 * (m8 // 2 + 1))
        if q == 0:
            vector.wait_ge(s_pem, 2 * (c - 2) + 1)
        with nc.allow_low_precision(reason="int mask sums <=32 exact in bf16"):
            vector.tensor_reduce(
                out=colsum[par][:, q * 32:(q + 1) * 32].rearrange("p (a b) -> p a b", a=4, b=8),
                in_=mask_dbp[m8 % 2][:].rearrange("p (a b c) -> p a b c", a=4, b=8, c=32),
                axis=AX.X, op=ALU.add,
            ).then_inc(s_mred2, 1)

    def mask_pe(tensor, c):
        par = c % 2
        if c <= 1:
            tensor.wait_ge(s_mred, 16 * (c + 1))
        else:
            tensor.wait_ge(s_mred2, 8 * (c - 1))
        if c >= 2:
            tensor.wait_ge(s_bm, c - 1)
        tensor.matmul(ps_m[0:4, 0:256], g32[:], colsum[par][:], start=True, stop=True).then_inc(s_pem, 1)

    def mask_pe2(tensor, c):
        par = c % 2
        tensor.wait_ge(s_bm, c + 1)
        if c >= 2:
            tensor.wait_ge(s_emT, c - 1)
        tensor.matmul(ps_e[:, 0:256], rep4[:], bm[par][:], start=True, stop=True).then_inc(s_pem, 1)

    def mask_min(vector, c):
        par = c % 2
        vector.wait_ge(s_pem, 2 * c + 1)
        with nc.allow_low_precision(reason="0/1 threshold exact in bf16"):
            vector.tensor_scalar_min(out=bm[par][:], in0=ps_m[0:4, 0:256], scalar1=1.0).then_inc(s_bm, 1)

    def mask_evac(vector, c):
        par = c % 2
        vector.wait_ge(s_pem, 2 * c + 2)
        with nc.allow_low_precision(reason="0/1 mask exact in bf16"):
            vector.tensor_copy(out=emT[par][:], in_=ps_e[:, 0:256]).then_inc(s_emT, 1)

    def wm_op(vector, c, g):
        gi = 4 * c + g
        vector.wait_ge(s_wgt[gi % 2], 16 * (gi // 2 + 1))
        vector.wait_ge(s_emT, c + 1)
        wg = wgt_db[gi % 2][:].rearrange("p (a b c) -> p a b c", a=8, b=8, c=32)
        emb = emT[c % 2][:, g * 64:(g + 1) * 64] \
            .rearrange("p (a b) -> p a b", a=8, b=8).unsqueeze(3).broadcast_to((128, 8, 8, 32))
        vector.tensor_tensor(out=wg, in0=wg, in1=emb, op=ALU.mult).then_inc(s_wm, 1)

    def add_op(vector, c, g):
        pc = c % 2
        vector.wait_ge(s_wm, c * 4 + g + 1)
        vector.wait_ge(s_wW[pc][g], 16 * (c // 2 + 1))
        vector.tensor_tensor(
            out=wsum[pc][:, g * 8 * CW:(g + 1) * 8 * CW],
            in0=wsum[pc][:, g * 8 * CW:(g + 1) * 8 * CW],
            in1=wgt_db[(4 * c + g) % 2][:],
            op=ALU.add,
        ).then_inc(s_add, 1)

    with nc.Block() as block:

        @block.sync
        def _(sync):
            sync.dma_start(out=i128[:], in_=id_h).then_inc(s_cst, 16)
            sync.dma_start(out=g32[:], in_=g32_h).then_inc(s_cst, 16)
            sync.dma_start(out=rep4[:], in_=rep4_h).then_inc(s_cst, 16)
            sync.dma_start(out=ones_sb[:], in_=ones_h).then_inc(s_cst, 16)
            sync.dma_start(out=lw_sb[:], in_=lw_h).then_inc(s_cst, 16)
            mask_dma(sync, 0)
            mask_dma(sync, 1)
            mask_dma_main(sync, 2)
            for c in range(NCH):
                if c + 3 < NCH:
                    mask_dma_main(sync, c + 3)

        @block.gpsimd
        def _(gpsimd):
            def wgt_dma(c, g):
                gi = 4 * c + g
                if c == 0 and g == 0:
                    gpsimd.wait_ge(s_dvc, ln_idx[(12, "act")])
                    gpsimd.wait_ge(s_sq, 13)
                    gpsimd.wait_ge(s_peg, 4 * 13)
                if gi >= 2:
                    gpsimd.wait_ge(s_add, gi - 1)
                gpsimd.dma_start(
                    out=wgt_db[gi % 2][:].rearrange("p (a b) -> p a b", a=8, b=CW),
                    in_=wgt_h[:, c * CW:(c + 1) * CW]
                        .rearrange("(kt p) n -> p kt n", p=128)[:, g * 8:(g + 1) * 8, :],
                ).then_inc(s_wgt[gi % 2], 16)

            def w_dma(c, g):
                pc = c % 2
                if c == 1:
                    gpsimd.wait_ge(s_dvc, ln_idx[(NTT - 1, "act")])
                    gpsimd.wait_ge(s_peg, 4 * NTT)
                elif c >= 2:
                    gpsimd.wait_ge(s_pmain, (c - 1) * NTT)
                gpsimd.dma_start(
                    out=wsum[pc][:, g * 8 * CW:(g + 1) * 8 * CW]
                        .rearrange("p (a b) -> p a b", a=8, b=CW),
                    in_=wt_h[:, c * CW:(c + 1) * CW]
                        .rearrange("(kt p) n -> p kt n", p=128)[:, g * 8:(g + 1) * 8, :],
                ).then_inc(s_wW[pc][g], 16)

            for t in range(NTT):
                tp = t - 3 if t < 14 else t - 2
                if t >= 3:
                    gpsimd.wait_ge(s_dvc, ln_idx[(tp, "act")])
                    gpsimd.wait_ge(s_sq, tp + 1)
                    gpsimd.wait_ge(s_peg, 4 * tp + 2)
                gpsimd.dma_start(out=x_bufs[xb(t)][:, 0:2048],
                                 in_=x_h[t * 128:(t + 1) * 128, 0:2048]).then_inc(s_x[xb(t)], 16)
                if t >= 3:
                    gpsimd.wait_ge(s_peg, 4 * tp + 4)
                gpsimd.dma_start(out=x_bufs[xb(t)][:, 2048:4096],
                                 in_=x_h[t * 128:(t + 1) * 128, 2048:4096]).then_inc(s_x[xb(t)], 16)
                if t == 0:
                    for i in range(2):
                        gpsimd.dma_start(
                            out=at_bf[:, i * 2048:(i + 1) * 2048].rearrange("p (a b) -> p a b", a=16, b=128),
                            in_=at_h.rearrange("(kt p) a -> p kt a", p=128)[:, i * 16:(i + 1) * 16, :],
                        ).then_inc(s_at, 16)
                elif 2 <= t <= 5:
                    w_dma(0, t - 2)
                elif t == 6:
                    gpsimd.dma_start(out=cbt[0][:], in_=bt_h[:, 0:CW]).then_inc(s_cb[0], 16)
                    gpsimd.dma_start(out=bbf[0][0:1, :],
                                     in_=bias_h[0:CW].unsqueeze(0)).then_inc(s_bb[0], 16)
            for g in range(4):
                wgt_dma(0, g)

            for c in range(1, NCH):
                pc = c % 2
                wgt_dma(c, 0)
                wgt_dma(c, 1)
                for g in range(4):
                    w_dma(c, g)
                if c >= 2:
                    gpsimd.wait_ge(s_pmain, (c - 1) * NTT)
                gpsimd.dma_start(out=cbt[pc][:], in_=bt_h[:, c * CW:(c + 1) * CW]).then_inc(s_cb[pc], 16)
                gpsimd.dma_start(out=bbf[pc][0:1, :],
                                 in_=bias_h[c * CW:(c + 1) * CW].unsqueeze(0)).then_inc(s_bb[pc], 16)
                wgt_dma(c, 2)
                wgt_dma(c, 3)
                if debug and c == 1:
                    gpsimd.wait_ge(s_sev, 4 * NTT)
                    gpsimd.dma_start(out=dbg_xnT, in_=xnT[:, 0:2048]).then_inc(s_dbg, 16)
                    gpsimd.wait_ge(s_emT, 1)
                    gpsimd.dma_start(out=dbg_emT, in_=emT[0][:]).then_inc(s_dbg, 16)
                    gpsimd.wait_ge(s_hrdy, 4)
                    gpsimd.dma_start(out=dbg_h, in_=h_sb[:]).then_inc(s_dbg, 16)
                    gpsimd.wait_ge(s_add, 4)
                    gpsimd.dma_start(out=dbg_ws, in_=wsum[0][:]).then_inc(s_dbg, 16)
                    gpsimd.wait_ge(s_cv, 1)
                    gpsimd.dma_start(out=dbg_cv, in_=cvec[:]).then_inc(s_dbg, 16)
                    gpsimd.wait_ge(s_dbg, 80)

        @block.vector
        def _(vector):
            nd = 0

            vector.wait_ge(s_cst, 5 * 16)
            vector.tensor_scalar_mul(out=cvec[:], in0=lw_sb[:], scalar1=2.0).then_inc(s_cv, 1)

            def dv(ins, tile=None, name=None):
                nonlocal nd
                if nd:
                    vector.wait_ge(s_dvc, nd)
                r = ins()
                nd += 1
                r.then_inc(s_dvc, 1)
                if name is not None:
                    assert ln_idx[(tile, name)] == nd, (tile, name, nd)

            def chain_a(t):
                p = t % 2
                vector.wait_ge(s_id, t + 1)
                dv(lambda: vector.tensor_scalar_mul(out=mean[p][:], in0=s1[p][:], scalar1=1.0 / D),
                   t, "mean")
                vector.wait_ge(s_sq, t + 1)
                dv(lambda: vector.tensor_scalar(out=vareps[p][:], in0=s2[p][:], scalar1=1.0 / D,
                                                scalar2=EPS, op0=ALU.mult, op1=ALU.add),
                   t, "vareps")
                dv(lambda: vector.tensor_mul(out=m2[p][:], in0=mean[p][:], in1=mean[p][:]))
                dv(lambda: vector.tensor_sub(out=veps[p][:], in0=vareps[p][:], in1=m2[p][:]))
                dv(lambda: vector.reciprocal(out=rr[p][:], in_=veps[p][:]), t, "rr")

            def chain_b(t):
                p = t % 2
                vector.wait_ge(s_r0, t + 1)
                dv(lambda: vector.scalar_tensor_tensor(out=t1[p][:], in0=r0[p][:], scalar=r0[p][:],
                                                       in1=veps[p][:], op0=ALU.mult, op1=ALU.mult))
                dv(lambda: vector.tensor_scalar(out=t3[p][:], in0=t1[p][:], scalar1=-0.5,
                                                scalar2=1.5, op0=ALU.mult, op1=ALU.add))
                dv(lambda: vector.tensor_mul(out=rstd[p][:], in0=r0[p][:], in1=t3[p][:]), t, "rstd")
                dv(lambda: vector.scalar_tensor_tensor(out=nmr[p][:], in0=mean[p][:], scalar=-1.0,
                                                       in1=rstd[p][:], op0=ALU.mult, op1=ALU.mult))
                with nc.allow_low_precision(reason="xn is a bf16 matmul operand"):
                    dv(lambda: vector.tensor_scalar(out=x_bufs[xb(t)][:], in0=x_bufs[xb(t)][:],
                                                    scalar1=rstd[p][:], scalar2=nmr[p][:],
                                                    op0=ALU.mult, op1=ALU.add), t, "act")

            for t in range(NTT + 1):
                if t >= 1:
                    for sg in range(4):
                        g = 4 * (t - 1) + sg
                        vector.wait_ge(s_peg, g + 1)
                        vector.tensor_copy(
                            out=xnT[:].rearrange("p (kt tt) -> p kt tt", kt=NKT)
                                [:, 8 * sg:8 * sg + 8, (t - 1) * 128:t * 128],
                            in_=ps_x[g % 2][:].rearrange("p (a b) -> p a b", a=8, b=128),
                        ).then_inc(s_sev, 1)
                if t < NTT:
                    chain_a(t)
                    chain_b(t)
                if t <= 15:
                    mask_reduce(vector, (2 * t) // 16, (2 * t) % 16)
                    mask_reduce(vector, (2 * t + 1) // 16, (2 * t + 1) % 16)
                if t == 9:
                    mask_min(vector, 0)
                elif t == 10:
                    mask_evac(vector, 0)
                elif t == 16:
                    mask_min(vector, 1)
                    mask_evac(vector, 1)
            assert nd == LNOPS * NTT, nd
            for g in range(4):
                wm_op(vector, 0, g)
                add_op(vector, 0, g)

            for c in range(NCH):
                for tt in range(NTT):
                    mm = c * NTT + tt
                    og = c * 4 + tt // 4
                    if c == 0:
                        if tt < 8:
                            mask_reduce_main(vector, 2, tt)
                        elif tt == 9:
                            mask_min(vector, 2)
                        elif tt == 10:
                            mask_evac(vector, 2)
                    elif c + 2 < NCH:
                        if tt == 0:
                            mask_min(vector, c + 2)
                        elif tt == 1:
                            mask_evac(vector, c + 2)
                    vector.wait_ge(s_pmain, mm + 1)
                    if og < 2:
                        vector.wait_ge(s_sq, NTT)
                    vector.wait_ge(s_rsd[og % 2], 16 * (og // 2 + 1))
                    if og >= 2 and tt % 4 == 0:
                        vector.wait_ge(s_out[og % 2], 16 * (og // 2))
                    vector.scalar_tensor_tensor(
                        out=out_db[og % 2][:, (tt % 4) * CW:(tt % 4 + 1) * CW],
                        in0=ps_o[mm % 2][:, 0:CW], scalar=1.0,
                        in1=resid_db[og % 2][:, (tt % 4) * CW:(tt % 4 + 1) * CW],
                        op0=ALU.mult, op1=ALU.add,
                    ).then_inc(s_ev, 1)
                    if c + 1 < NCH and 2 <= tt < 10:
                        g = (tt - 2) // 2
                        if tt % 2 == 0:
                            wm_op(vector, c + 1, g)
                        else:
                            add_op(vector, c + 1, g)
                    if c + 3 < NCH and 8 <= tt < 16:
                        mask_reduce_main(vector, c + 3, tt - 8)

        @block.tensor
        def _(tensor):
            tensor.wait_ge(s_cst, 5 * 16)
            for t in range(NTT):
                tensor.wait_ge(s_dvc, ln_idx[(t, "act")])
                for sg in range(4):
                    g = 4 * t + sg
                    if g >= 2:
                        tensor.wait_ge(s_sev, g - 1)
                    for j in range(8):
                        ins = tensor.transpose(ps_x[g % 2][:, j * 128:(j + 1) * 128],
                                               x_bufs[xb(t)][:, (8 * sg + j) * 128:(8 * sg + j + 1) * 128],
                                               i128[:])
                        if j == 7:
                            ins.then_inc(s_peg, 1)
                if t == 15:
                    mask_pe(tensor, 1)
                if t % 4 == 3:
                    tc = t // 4
                    tensor.wait_ge(s_sev, 4 * (t + 1))
                    tensor.wait_ge(s_at, 32)
                    if tc >= 2:
                        tensor.wait_ge(s_hrdy, tc - 1)
                    for kt in range(NKT):
                        ins = tensor.matmul(ps_h[tc % 2][:, 0:512],
                                            at_bf[:, kt * 128:(kt + 1) * 128],
                                            xnT[:, kt * T + tc * 512:kt * T + tc * 512 + 512],
                                            start=(kt == 0), stop=(kt == NKT - 1))
                        if kt == NKT - 1:
                            ins.then_inc(s_peh, 1)
                if t == 8:
                    mask_pe(tensor, 0)
                elif t == 9:
                    mask_pe2(tensor, 0)
                elif t == 15:
                    mask_pe2(tensor, 1)
            tensor.wait_ge(s_hrdy, 4)
            for c in range(NCH):
                pc = c % 2
                tensor.wait_ge(s_add, (c + 1) * 4)
                tensor.wait_ge(s_cb[pc], 16 * (c // 2 + 1))
                tensor.wait_ge(s_bb[pc], 16 * (c // 2 + 1))
                if c >= 1 and c + 2 < NCH:
                    mask_pe(tensor, c + 2)
                for tt in range(NTT):
                    mm = c * NTT + tt
                    if mm >= 2:
                        tensor.wait_ge(s_ev, mm - 1)
                    for kt in range(NKT):
                        tensor.matmul(ps_o[mm % 2][:, 0:CW],
                                      xnT[:, kt * T + tt * 128:kt * T + tt * 128 + 128],
                                      wsum[pc][:, kt * CW:(kt + 1) * CW],
                                      start=(kt == 0), stop=False)
                    tensor.matmul(ps_o[mm % 2][:, 0:CW],
                                  h_sb[:, tt * 128:(tt + 1) * 128],
                                  cbt[pc][:], start=False, stop=False)
                    tensor.matmul(ps_o[mm % 2][:, 0:CW],
                                  ones_sb[:], bbf[pc][0:1, :],
                                  start=False, stop=True).then_inc(s_pmain, 1)
                    if c == 0:
                        if tt == 8:
                            mask_pe(tensor, 2)
                        elif tt == 9:
                            mask_pe2(tensor, 2)
                    elif c + 2 < NCH and tt == 0:
                        mask_pe2(tensor, c + 2)

        @block.scalar
        def _(scalar):
            for t in range(NTT):
                nfill = t // 3 + 1 if t < 14 else 6
                scalar.wait_ge(s_x[xb(t)], 32 * nfill)
                if t >= 1:
                    scalar.wait_ge(s_sq, t)
                if t >= 2:
                    scalar.wait_ge(s_dvc, ln_idx[(t - 2, "mean")])
                scalar.activation(out=sq_scr[:], in_=x_bufs[xb(t)][:], func=AF.Identity,
                                  accum_out=s1[t % 2][:]).then_inc(s_id, 1)
                scalar.wait_ge(s_id, t + 1)
                if t >= 2:
                    scalar.wait_ge(s_dvc, ln_idx[(t - 2, "vareps")])
                scalar.activation(out=sq_scr[:], in_=x_bufs[xb(t)][:], func=AF.Square,
                                  accum_out=s2[t % 2][:]).then_inc(s_sq, 1)
                scalar.wait_ge(s_dvc, ln_idx[(t, "rr")])
                if t >= 2:
                    scalar.wait_ge(s_dvc, ln_idx[(t - 2, "rstd")])
                scalar.activation(out=r0[t % 2][:], in_=rr[t % 2][:], func=AF.Sqrt).then_inc(s_r0, 1)
                if t % 4 == 1 and t >= 5:
                    tc = (t - 5) // 4
                    if tc == 0:
                        scalar.wait_ge(s_cv, 1)
                    scalar.wait_ge(s_peh, tc + 1)
                    scalar.activation(out=h_sb[:, tc * 512:(tc + 1) * 512], in_=ps_h[tc % 2][:, 0:512],
                                      func=AF.Identity, scale=cvec[:]).then_inc(s_hrdy, 1)
            scalar.wait_ge(s_peh, 4)
            scalar.activation(out=h_sb[:, 3 * 512:4 * 512], in_=ps_h[1][:, 0:512],
                              func=AF.Identity, scale=cvec[:]).then_inc(s_hrdy, 1)
            def resid_dma(og):
                c2, q2 = og // 4, og % 4
                if og < 2:
                    scalar.wait_ge(s_peh, 4)
                scalar.dma_start(
                    out=resid_db[og % 2][:].rearrange("p (j n) -> p j n", j=4),
                    in_=x_h[q2 * 512:(q2 + 1) * 512, c2 * CW:(c2 + 1) * CW]
                        .rearrange("(j p) n -> p j n", p=128),
                ).then_inc(s_rsd[og % 2], 16)

            resid_dma(0)
            resid_dma(1)
            for og in range(4 * NCH):
                c, q = og // 4, og % 4
                scalar.wait_ge(s_ev, (og + 1) * 4)
                if og + 2 < 4 * NCH:
                    resid_dma(og + 2)
                scalar.dma_start(
                    out=out_h[q * 512:(q + 1) * 512, c * CW:(c + 1) * CW]
                        .rearrange("(j p) n -> p j n", p=128),
                    in_=out_db[og % 2][:].rearrange("p (j n) -> p j n", j=4),
                ).then_inc(s_out[og % 2], 16)
            scalar.wait_ge(s_out[0], 16 * 2 * NCH)
            scalar.wait_ge(s_out[1], 16 * 2 * NCH)

    return nc


def _consts():
    ident = np.eye(128, dtype=ml_dtypes.bfloat16)
    g32 = np.zeros((128, 4), dtype=ml_dtypes.bfloat16)
    rep4 = np.zeros((4, 128), dtype=ml_dtypes.bfloat16)
    for n in range(128):
        g32[n, n // 32] = 1
        rep4[n // 32, n] = 1
    ones_row = np.ones((1, 128), dtype=ml_dtypes.bfloat16)
    return ident, g32, rep4, ones_row


def kernel(L0_attn_data, L0_attn_weight, L0_attn_mask, L0_ffn_lora_A, L0_ffn_lora_B,
           L0_ffn_W, L0_ffn_bias, L0_ffn_lora_weights):
    x = np.asarray(L0_attn_data, dtype=np.float32)
    wgtT = np.ascontiguousarray(np.asarray(L0_attn_weight, dtype=np.float32).T)
    maskT = np.ascontiguousarray(np.asarray(L0_attn_mask, dtype=np.int32).T)
    WT = np.ascontiguousarray(np.asarray(L0_ffn_W, dtype=np.float32).T)
    A_T = np.ascontiguousarray(np.asarray(L0_ffn_lora_A, dtype=np.float32).reshape(128, D).T)
    B_t = np.ascontiguousarray(np.asarray(L0_ffn_lora_B, dtype=np.float32).transpose(0, 2, 1).reshape(128, D))
    bias = np.ascontiguousarray(np.asarray(L0_ffn_bias, dtype=np.float32))
    lw_rep = np.ascontiguousarray(
        np.repeat(np.asarray(L0_ffn_lora_weights, dtype=np.float32), 16).reshape(128, 1))
    ident, g32, rep4, ones_row = _consts()

    nc = build_nc()
    in_maps = []
    for c in range(8):
        xs = np.ascontiguousarray(x[:, 1024 * c:1024 * (c + 1), :].reshape(T, D))
        in_maps.append(dict(x=xs, wt=WT, wgt=wgtT, maskt=maskT, at=A_T, bt=B_t,
                            bias=bias, lwrep=lw_rep, ident=ident, g32=g32,
                            rep4=rep4, onesrow=ones_row))
    res = run_bass_kernel_spmd(nc, in_maps, core_ids=list(range(8)))
    out = np.empty((2, 8192, D), dtype=np.float32)
    for c in range(8):
        out[:, 1024 * c:1024 * (c + 1), :] = np.asarray(res.results[c]["out"]).reshape(2, 1024, D)
    return out


# version 28
# speedup vs baseline: 1.0595x; 1.0595x over previous
import numpy as np
import ml_dtypes
from contextlib import ExitStack

import concourse.bass as bass
import concourse.mybir as mybir
from concourse.bass_utils import run_bass_kernel_spmd

F32 = mybir.dt.float32
BF16 = mybir.dt.bfloat16
I32 = mybir.dt.int32
AF = mybir.ActivationFunctionType
ALU = mybir.AluOpType
AX = mybir.AxisListType

D = 4096
T = 2048
NKT = 32
NCH = 16
CW = 256
NTT = 16
EPS = 1e-5
LNOPS = 10


def build_nc(debug=False):
    nc = bass.Bass("TRN2", target_bir_lowering=False)

    x_h = nc.dram_tensor("x", [T, D], F32, kind="ExternalInput").ap()
    wt_h = nc.dram_tensor("wt", [D, D], F32, kind="ExternalInput").ap()
    wgt_h = nc.dram_tensor("wgt", [D, D], F32, kind="ExternalInput").ap()
    mask_h = nc.dram_tensor("maskt", [D, D], I32, kind="ExternalInput").ap()
    at_h = nc.dram_tensor("at", [D, 128], F32, kind="ExternalInput").ap()
    bt_h = nc.dram_tensor("bt", [128, D], F32, kind="ExternalInput").ap()
    bias_h = nc.dram_tensor("bias", [D], F32, kind="ExternalInput").ap()
    lw_h = nc.dram_tensor("lwrep", [128, 1], F32, kind="ExternalInput").ap()
    id_h = nc.dram_tensor("ident", [128, 128], BF16, kind="ExternalInput").ap()
    g32_h = nc.dram_tensor("g32", [128, 4], BF16, kind="ExternalInput").ap()
    rep4_h = nc.dram_tensor("rep4", [4, 128], BF16, kind="ExternalInput").ap()
    ones_h = nc.dram_tensor("onesrow", [1, 128], BF16, kind="ExternalInput").ap()
    out_h = nc.dram_tensor("out", [T, D], F32, kind="ExternalOutput").ap()
    if debug:
        dbg_xnT = nc.dram_tensor("dbg_xnT", [128, 2048], F32, kind="ExternalOutput").ap()
        dbg_emT = nc.dram_tensor("dbg_emT", [128, 256], F32, kind="ExternalOutput").ap()
        dbg_h = nc.dram_tensor("dbg_h", [128, T], F32, kind="ExternalOutput").ap()
        dbg_ws = nc.dram_tensor("dbg_ws", [128, NKT * CW], F32, kind="ExternalOutput").ap()
        dbg_cv = nc.dram_tensor("dbg_cv", [128, 1], F32, kind="ExternalOutput").ap()

    A = nc.alloc_sbuf_tensor_at
    off = nc.sbuf_base

    def at_(name, shape, dtype, nbytes=None, align=32):
        nonlocal off
        off = (off + align - 1) // align * align
        t = A(name, shape, dtype, offset=off)
        sz = nbytes
        if sz is None:
            n = 1
            for s in shape[1:]:
                n *= s
            sz = n * mybir.dt.size(dtype)
        off += sz
        return t

    xnT = at_("xnT", [128, NKT * T], BF16)
    h_sb = at_("h_sb", [128, T], BF16)
    off = (off + 31) // 32 * 32
    _mask_off = off
    at_("maskraw", [128, 8 * CW], I32)
    mask_db = [A("maskdb%d" % i, [128, 2 * CW], I32, offset=_mask_off + i * 2048)
               for i in range(4)]
    mask_dbp = [A("maskdbp%d" % i, [128, 4 * CW], I32, offset=_mask_off + i * 4096)
                for i in range(2)]
    i128 = at_("i128", [128, 128], BF16)
    g32 = at_("g32s", [128, 4], BF16)
    rep4 = at_("rep4s", [4, 128], BF16, nbytes=256)
    ones_sb = at_("ones_sb", [1, 128], BF16, nbytes=256)
    lw_sb = at_("lw_sb", [128, 1], F32)
    cvec = at_("cvec", [128, 1], F32)
    s1 = [at_("s1%d" % i, [128, 1], F32) for i in range(2)]
    s2 = [at_("s2%d" % i, [128, 1], F32) for i in range(2)]
    mean = [at_("mean%d" % i, [128, 1], F32) for i in range(2)]
    vareps = [at_("vareps%d" % i, [128, 1], F32) for i in range(2)]
    m2 = [at_("m2%d" % i, [128, 1], F32) for i in range(2)]
    veps = [at_("veps%d" % i, [128, 1], F32) for i in range(2)]
    rr = [at_("rr%d" % i, [128, 1], F32) for i in range(2)]
    r0 = [at_("r0%d" % i, [128, 1], F32) for i in range(2)]
    t1 = [at_("t1%d" % i, [128, 1], F32) for i in range(2)]
    t3 = [at_("t3%d" % i, [128, 1], F32) for i in range(2)]
    rstd = [at_("rstd%d" % i, [128, 1], F32) for i in range(2)]
    nmr = [at_("nmr%d" % i, [128, 1], F32) for i in range(2)]
    ln_idx = {}
    _nd0 = 0
    for _t in range(NTT):
        for _nm in ("mean", "vareps", "m2", "veps", "rr",
                    "t1", "t3", "rstd", "nmr", "act"):
            _nd0 += 1
            ln_idx[(_t, _nm)] = _nd0
    assert _nd0 == LNOPS * NTT
    colsum = [at_("colsum%d" % i, [128, 256], BF16) for i in range(2)]
    bm = [at_("bm%d" % i, [4, 256], BF16, nbytes=512) for i in range(2)]
    emT = [at_("emT%d" % i, [128, 256], BF16) for i in range(2)]
    cbt = [at_("cbt%d" % i, [128, CW], BF16) for i in range(2)]
    bbf = [at_("bbf%d" % i, [1, CW], BF16, nbytes=512) for i in range(2)]

    arena = (off + 255) // 256 * 256
    wsum = [A("wsum%d" % i, [128, NKT * CW], BF16, offset=arena + i * 16384) for i in range(2)]
    wgt_db = [A("wgtdb%d" % i, [128, 8 * CW], BF16, offset=arena + 32768 + i * 4096) for i in range(2)]
    resid_db = [A("resid%d" % i, [128, 4 * CW], F32, offset=arena + 40960 + i * 4096) for i in range(2)]
    out_db = [A("outdb%d" % i, [128, 4 * CW], F32, offset=arena + 49152 + i * 4096) for i in range(2)]
    main_end = arena + 57344
    x_bufs = [A("x_buf%d" % i, [128, D], BF16, offset=arena + 16384 + i * 8192)
              for i in range(3)]

    def xb(t):
        return t % 3 if t < 14 else t - 14
    at_bf = A("at_bf", [128, NKT * 128], BF16, offset=arena + 40960)
    sq_scr = A("sq_scr", [128, D], BF16, offset=arena + 49152)
    assert main_end <= nc.sbuf_top, (main_end, nc.sbuf_top)

    ps_o = [nc.alloc_psum_tensor("ps_o%d" % i, [128, 512], F32) for i in range(2)]
    ps_x = [nc.alloc_psum_tensor("ps_x%d" % i, [128, 1024], BF16) for i in range(2)]
    ps_h = [nc.alloc_psum_tensor("ps_h%d" % i, [128, 512], F32) for i in range(2)]
    ps_m = nc.alloc_psum_tensor("ps_m", [128, 512], F32)
    ps_e = nc.alloc_psum_tensor("ps_e", [128, 512], F32)

    ctx = ExitStack()

    def sem(name):
        return ctx.enter_context(nc.semaphore(name))

    s_cst = sem("s_cst")
    s_x = [sem("s_x0"), sem("s_x1"), sem("s_x2")]
    s_msk = [sem("s_msk%d" % i) for i in range(4)]
    s_mskp = [sem("s_mskp%d" % i) for i in range(2)]
    s_mred2 = sem("s_mred2")
    s_rsd = [sem("s_rsd0"), sem("s_rsd1")]
    s_at = sem("s_at")
    s_wgt = [sem("s_wgt0"), sem("s_wgt1")]
    s_wW = [[sem("s_wW%d_%d" % (i, g)) for g in range(4)] for i in range(2)]
    s_cb = [sem("s_cb0"), sem("s_cb1")]
    s_bb = [sem("s_bb0"), sem("s_bb1")]
    s_mred = sem("s_mred")
    s_pem = sem("s_pem")
    s_bm = sem("s_bm")
    s_emT = sem("s_emT")
    s_sq = sem("s_sq")
    s_id = sem("s_id")
    s_dvc = sem("s_dvc")
    s_r0 = sem("s_r0")

    s_peg = sem("s_peg")
    s_sev = sem("s_sev")
    s_peh = sem("s_peh")
    s_hrdy = sem("s_hrdy")
    s_cv = sem("s_cv")
    s_wm = sem("s_wm")
    s_add = sem("s_add")
    s_pmain = sem("s_pmain")
    s_ev = sem("s_ev")
    s_out = [sem("s_out0"), sem("s_out1")]
    s_dbg = sem("s_dbg") if debug else None

    def mask_dma(eng, c, qs=tuple(range(16))):
        for q in qs:
            m = c * 16 + q
            if m >= 4:
                eng.wait_ge(s_mred, m - 3)
            eng.dma_start(
                out=mask_db[m % 4][:].rearrange("p (a b) -> p a b", a=2, b=CW),
                in_=mask_h[:, c * CW:(c + 1) * CW]
                    .rearrange("(kt p) n -> p kt n", p=128)[:, q * 2:(q + 1) * 2, :],
            ).then_inc(s_msk[m % 4], 16)

    def mask_dma_main(eng, c, qs=tuple(range(8))):
        for q in qs:
            m8 = 8 * (c - 2) + q
            if m8 == 0:
                eng.wait_ge(s_mred, 30)
            elif m8 == 1:
                eng.wait_ge(s_mred, 32)
            else:
                eng.wait_ge(s_mred2, m8 - 1)
            eng.dma_start(
                out=mask_dbp[m8 % 2][:].rearrange("p (a b) -> p a b", a=4, b=CW),
                in_=mask_h[:, c * CW:(c + 1) * CW]
                    .rearrange("(kt p) n -> p kt n", p=128)[:, q * 4:(q + 1) * 4, :],
            ).then_inc(s_mskp[m8 % 2], 16)

    def mask_reduce(vector, c, q):
        m = c * 16 + q
        par = c % 2
        vector.wait_ge(s_msk[m % 4], 16 * (m // 4 + 1))
        with nc.allow_low_precision(reason="int mask sums <=32 exact in bf16"):
            vector.tensor_reduce(
                out=colsum[par][:, q * 16:(q + 1) * 16].rearrange("p (a b) -> p a b", a=2, b=8),
                in_=mask_db[m % 4][:].rearrange("p (a b c) -> p a b c", a=2, b=8, c=32),
                axis=AX.X, op=ALU.add,
            ).then_inc(s_mred, 1)

    def mask_reduce_main(vector, c, q):
        m8 = 8 * (c - 2) + q
        par = c % 2
        vector.wait_ge(s_mskp[m8 % 2], 16 * (m8 // 2 + 1))
        if q == 0:
            vector.wait_ge(s_pem, 2 * (c - 2) + 1)
        with nc.allow_low_precision(reason="int mask sums <=32 exact in bf16"):
            vector.tensor_reduce(
                out=colsum[par][:, q * 32:(q + 1) * 32].rearrange("p (a b) -> p a b", a=4, b=8),
                in_=mask_dbp[m8 % 2][:].rearrange("p (a b c) -> p a b c", a=4, b=8, c=32),
                axis=AX.X, op=ALU.add,
            ).then_inc(s_mred2, 1)

    def mask_pe(tensor, c):
        par = c % 2
        if c <= 1:
            tensor.wait_ge(s_mred, 16 * (c + 1))
        else:
            tensor.wait_ge(s_mred2, 8 * (c - 1))
        if c >= 2:
            tensor.wait_ge(s_bm, c - 1)
        tensor.matmul(ps_m[0:4, 0:256], g32[:], colsum[par][:], start=True, stop=True).then_inc(s_pem, 1)

    def mask_pe2(tensor, c):
        par = c % 2
        tensor.wait_ge(s_bm, c + 1)
        if c >= 2:
            tensor.wait_ge(s_emT, c - 1)
        tensor.matmul(ps_e[:, 0:256], rep4[:], bm[par][:], start=True, stop=True).then_inc(s_pem, 1)

    def mask_min(vector, c):
        par = c % 2
        vector.wait_ge(s_pem, 2 * c + 1)
        with nc.allow_low_precision(reason="0/1 threshold exact in bf16"):
            vector.tensor_scalar_min(out=bm[par][:], in0=ps_m[0:4, 0:256], scalar1=1.0).then_inc(s_bm, 1)

    def mask_evac(vector, c):
        par = c % 2
        vector.wait_ge(s_pem, 2 * c + 2)
        with nc.allow_low_precision(reason="0/1 mask exact in bf16"):
            vector.tensor_copy(out=emT[par][:], in_=ps_e[:, 0:256]).then_inc(s_emT, 1)

    def wm_op(vector, c, g):
        gi = 4 * c + g
        vector.wait_ge(s_wgt[gi % 2], 16 * (gi // 2 + 1))
        vector.wait_ge(s_emT, c + 1)
        wg = wgt_db[gi % 2][:].rearrange("p (a b c) -> p a b c", a=8, b=8, c=32)
        emb = emT[c % 2][:, g * 64:(g + 1) * 64] \
            .rearrange("p (a b) -> p a b", a=8, b=8).unsqueeze(3).broadcast_to((128, 8, 8, 32))
        vector.tensor_tensor(out=wg, in0=wg, in1=emb, op=ALU.mult).then_inc(s_wm, 1)

    def add_op(vector, c, g):
        pc = c % 2
        vector.wait_ge(s_wm, c * 4 + g + 1)
        vector.wait_ge(s_wW[pc][g], 16 * (c // 2 + 1))
        vector.tensor_tensor(
            out=wsum[pc][:, g * 8 * CW:(g + 1) * 8 * CW],
            in0=wsum[pc][:, g * 8 * CW:(g + 1) * 8 * CW],
            in1=wgt_db[(4 * c + g) % 2][:],
            op=ALU.add,
        ).then_inc(s_add, 1)

    with nc.Block() as block:

        @block.sync
        def _(sync):
            sync.dma_start(out=i128[:], in_=id_h).then_inc(s_cst, 16)
            sync.dma_start(out=g32[:], in_=g32_h).then_inc(s_cst, 16)
            sync.dma_start(out=rep4[:], in_=rep4_h).then_inc(s_cst, 16)
            sync.dma_start(out=ones_sb[:], in_=ones_h).then_inc(s_cst, 16)
            sync.dma_start(out=lw_sb[:], in_=lw_h).then_inc(s_cst, 16)
            mask_dma(sync, 0)
            mask_dma(sync, 1)
            mask_dma_main(sync, 2)
            for c in range(NCH):
                if c + 3 < NCH:
                    mask_dma_main(sync, c + 3)

        @block.gpsimd
        def _(gpsimd):
            def wgt_dma(c, g):
                gi = 4 * c + g
                if c == 0 and g == 0:
                    gpsimd.wait_ge(s_dvc, ln_idx[(12, "act")])
                    gpsimd.wait_ge(s_sq, 13)
                    gpsimd.wait_ge(s_peg, 4 * 13)
                if gi >= 2:
                    gpsimd.wait_ge(s_add, gi - 1)
                gpsimd.dma_start(
                    out=wgt_db[gi % 2][:].rearrange("p (a b) -> p a b", a=8, b=CW),
                    in_=wgt_h[:, c * CW:(c + 1) * CW]
                        .rearrange("(kt p) n -> p kt n", p=128)[:, g * 8:(g + 1) * 8, :],
                ).then_inc(s_wgt[gi % 2], 16)

            def w_dma(c, g):
                pc = c % 2
                if c == 1:
                    gpsimd.wait_ge(s_dvc, ln_idx[(NTT - 1, "act")])
                    gpsimd.wait_ge(s_peg, 4 * NTT)
                elif c >= 2:
                    gpsimd.wait_ge(s_pmain, (c - 1) * NTT)
                gpsimd.dma_start(
                    out=wsum[pc][:, g * 8 * CW:(g + 1) * 8 * CW]
                        .rearrange("p (a b) -> p a b", a=8, b=CW),
                    in_=wt_h[:, c * CW:(c + 1) * CW]
                        .rearrange("(kt p) n -> p kt n", p=128)[:, g * 8:(g + 1) * 8, :],
                ).then_inc(s_wW[pc][g], 16)

            for t in range(NTT):
                tp = t - 3 if t < 14 else t - 2
                if t >= 3:
                    gpsimd.wait_ge(s_dvc, ln_idx[(tp, "act")])
                    gpsimd.wait_ge(s_sq, tp + 1)
                    gpsimd.wait_ge(s_peg, 4 * tp + 2)
                gpsimd.dma_start(out=x_bufs[xb(t)][:, 0:2048],
                                 in_=x_h[t * 128:(t + 1) * 128, 0:2048]).then_inc(s_x[xb(t)], 16)
                if t >= 3:
                    gpsimd.wait_ge(s_peg, 4 * tp + 4)
                gpsimd.dma_start(out=x_bufs[xb(t)][:, 2048:4096],
                                 in_=x_h[t * 128:(t + 1) * 128, 2048:4096]).then_inc(s_x[xb(t)], 16)
                if t == 0:
                    for i in range(2):
                        gpsimd.dma_start(
                            out=at_bf[:, i * 2048:(i + 1) * 2048].rearrange("p (a b) -> p a b", a=16, b=128),
                            in_=at_h.rearrange("(kt p) a -> p kt a", p=128)[:, i * 16:(i + 1) * 16, :],
                        ).then_inc(s_at, 16)
                elif 2 <= t <= 5:
                    w_dma(0, t - 2)
                elif t == 6:
                    gpsimd.dma_start(out=cbt[0][:], in_=bt_h[:, 0:CW]).then_inc(s_cb[0], 16)
                    gpsimd.dma_start(out=bbf[0][0:1, :],
                                     in_=bias_h[0:CW].unsqueeze(0)).then_inc(s_bb[0], 16)
            for g in range(4):
                wgt_dma(0, g)

            for c in range(1, NCH):
                pc = c % 2
                wgt_dma(c, 0)
                wgt_dma(c, 1)
                for g in range(4):
                    w_dma(c, g)
                if c >= 2:
                    gpsimd.wait_ge(s_pmain, (c - 1) * NTT)
                gpsimd.dma_start(out=cbt[pc][:], in_=bt_h[:, c * CW:(c + 1) * CW]).then_inc(s_cb[pc], 16)
                gpsimd.dma_start(out=bbf[pc][0:1, :],
                                 in_=bias_h[c * CW:(c + 1) * CW].unsqueeze(0)).then_inc(s_bb[pc], 16)
                wgt_dma(c, 2)
                wgt_dma(c, 3)
                if debug and c == 1:
                    gpsimd.wait_ge(s_sev, 4 * NTT)
                    gpsimd.dma_start(out=dbg_xnT, in_=xnT[:, 0:2048]).then_inc(s_dbg, 16)
                    gpsimd.wait_ge(s_emT, 1)
                    gpsimd.dma_start(out=dbg_emT, in_=emT[0][:]).then_inc(s_dbg, 16)
                    gpsimd.wait_ge(s_hrdy, 4)
                    gpsimd.dma_start(out=dbg_h, in_=h_sb[:]).then_inc(s_dbg, 16)
                    gpsimd.wait_ge(s_add, 4)
                    gpsimd.dma_start(out=dbg_ws, in_=wsum[0][:]).then_inc(s_dbg, 16)
                    gpsimd.wait_ge(s_cv, 1)
                    gpsimd.dma_start(out=dbg_cv, in_=cvec[:]).then_inc(s_dbg, 16)
                    gpsimd.wait_ge(s_dbg, 80)

        @block.vector
        def _(vector):
            nd = 0

            vector.wait_ge(s_cst, 5 * 16)
            vector.tensor_scalar_mul(out=cvec[:], in0=lw_sb[:], scalar1=2.0).then_inc(s_cv, 1)

            def dv(ins, tile=None, name=None):
                nonlocal nd
                if nd:
                    vector.wait_ge(s_dvc, nd)
                r = ins()
                nd += 1
                r.then_inc(s_dvc, 1)
                if name is not None:
                    assert ln_idx[(tile, name)] == nd, (tile, name, nd)

            def chain_a(t):
                p = t % 2
                vector.wait_ge(s_id, t + 1)
                dv(lambda: vector.tensor_scalar_mul(out=mean[p][:], in0=s1[p][:], scalar1=1.0 / D),
                   t, "mean")
                vector.wait_ge(s_sq, t + 1)
                dv(lambda: vector.tensor_scalar(out=vareps[p][:], in0=s2[p][:], scalar1=1.0 / D,
                                                scalar2=EPS, op0=ALU.mult, op1=ALU.add),
                   t, "vareps")
                dv(lambda: vector.tensor_mul(out=m2[p][:], in0=mean[p][:], in1=mean[p][:]))
                dv(lambda: vector.tensor_sub(out=veps[p][:], in0=vareps[p][:], in1=m2[p][:]))
                dv(lambda: vector.reciprocal(out=rr[p][:], in_=veps[p][:]), t, "rr")

            def chain_b(t):
                p = t % 2
                vector.wait_ge(s_r0, t + 1)
                dv(lambda: vector.scalar_tensor_tensor(out=t1[p][:], in0=r0[p][:], scalar=r0[p][:],
                                                       in1=veps[p][:], op0=ALU.mult, op1=ALU.mult))
                dv(lambda: vector.tensor_scalar(out=t3[p][:], in0=t1[p][:], scalar1=-0.5,
                                                scalar2=1.5, op0=ALU.mult, op1=ALU.add))
                dv(lambda: vector.tensor_mul(out=rstd[p][:], in0=r0[p][:], in1=t3[p][:]), t, "rstd")
                dv(lambda: vector.scalar_tensor_tensor(out=nmr[p][:], in0=mean[p][:], scalar=-1.0,
                                                       in1=rstd[p][:], op0=ALU.mult, op1=ALU.mult))
                with nc.allow_low_precision(reason="xn is a bf16 matmul operand"):
                    dv(lambda: vector.tensor_scalar(out=x_bufs[xb(t)][:], in0=x_bufs[xb(t)][:],
                                                    scalar1=rstd[p][:], scalar2=nmr[p][:],
                                                    op0=ALU.mult, op1=ALU.add), t, "act")

            for t in range(NTT + 1):
                if t >= 1:
                    for sg in range(4):
                        g = 4 * (t - 1) + sg
                        vector.wait_ge(s_peg, g + 1)
                        vector.tensor_copy(
                            out=xnT[:].rearrange("p (kt tt) -> p kt tt", kt=NKT)
                                [:, 8 * sg:8 * sg + 8, (t - 1) * 128:t * 128],
                            in_=ps_x[g % 2][:].rearrange("p (a b) -> p a b", a=8, b=128),
                        ).then_inc(s_sev, 1)
                if t < NTT:
                    chain_a(t)
                    chain_b(t)
                if t <= 15:
                    mask_reduce(vector, (2 * t) // 16, (2 * t) % 16)
                    mask_reduce(vector, (2 * t + 1) // 16, (2 * t + 1) % 16)
                if t == 9:
                    mask_min(vector, 0)
                elif t == 10:
                    mask_evac(vector, 0)
                elif t == 16:
                    mask_min(vector, 1)
                    mask_evac(vector, 1)
            assert nd == LNOPS * NTT, nd
            for g in range(4):
                wm_op(vector, 0, g)
                add_op(vector, 0, g)

            for c in range(NCH):
                for tt in range(NTT):
                    mm = c * NTT + tt
                    og = c * 4 + tt // 4
                    if c == 0:
                        if tt < 8:
                            mask_reduce_main(vector, 2, tt)
                        elif tt == 9:
                            mask_min(vector, 2)
                        elif tt == 10:
                            mask_evac(vector, 2)
                    elif c + 2 < NCH:
                        if tt == 0:
                            mask_min(vector, c + 2)
                        elif tt == 1:
                            mask_evac(vector, c + 2)
                    vector.wait_ge(s_pmain, mm + 1)
                    if og < 2:
                        vector.wait_ge(s_sq, NTT)
                    vector.wait_ge(s_rsd[og % 2], 16 * (og // 2 + 1))
                    if og >= 2 and tt % 4 == 0:
                        vector.wait_ge(s_out[og % 2], 16 * (og // 2))
                    vector.scalar_tensor_tensor(
                        out=out_db[og % 2][:, (tt % 4) * CW:(tt % 4 + 1) * CW],
                        in0=ps_o[mm % 2][:, 0:CW], scalar=1.0,
                        in1=resid_db[og % 2][:, (tt % 4) * CW:(tt % 4 + 1) * CW],
                        op0=ALU.mult, op1=ALU.add,
                    ).then_inc(s_ev, 1)
                    if c + 1 < NCH and 2 <= tt < 10:
                        g = (tt - 2) // 2
                        if tt % 2 == 0:
                            wm_op(vector, c + 1, g)
                        else:
                            add_op(vector, c + 1, g)
                    if c + 3 < NCH and 7 <= tt < 15:
                        mask_reduce_main(vector, c + 3, tt - 7)

        @block.tensor
        def _(tensor):
            tensor.wait_ge(s_cst, 5 * 16)
            for t in range(NTT):
                tensor.wait_ge(s_dvc, ln_idx[(t, "act")])
                for sg in range(4):
                    g = 4 * t + sg
                    if g >= 2:
                        tensor.wait_ge(s_sev, g - 1)
                    for j in range(8):
                        ins = tensor.transpose(ps_x[g % 2][:, j * 128:(j + 1) * 128],
                                               x_bufs[xb(t)][:, (8 * sg + j) * 128:(8 * sg + j + 1) * 128],
                                               i128[:])
                        if j == 7:
                            ins.then_inc(s_peg, 1)
                if t == 15:
                    mask_pe(tensor, 1)
                if t % 4 == 3:
                    tc = t // 4
                    tensor.wait_ge(s_sev, 4 * (t + 1))
                    tensor.wait_ge(s_at, 32)
                    if tc >= 2:
                        tensor.wait_ge(s_hrdy, tc - 1)
                    for kt in range(NKT):
                        ins = tensor.matmul(ps_h[tc % 2][:, 0:512],
                                            at_bf[:, kt * 128:(kt + 1) * 128],
                                            xnT[:, kt * T + tc * 512:kt * T + tc * 512 + 512],
                                            start=(kt == 0), stop=(kt == NKT - 1))
                        if kt == NKT - 1:
                            ins.then_inc(s_peh, 1)
                if t == 8:
                    mask_pe(tensor, 0)
                elif t == 9:
                    mask_pe2(tensor, 0)
                elif t == 15:
                    mask_pe2(tensor, 1)
            tensor.wait_ge(s_hrdy, 4)
            for c in range(NCH):
                pc = c % 2
                tensor.wait_ge(s_add, (c + 1) * 4)
                tensor.wait_ge(s_cb[pc], 16 * (c // 2 + 1))
                tensor.wait_ge(s_bb[pc], 16 * (c // 2 + 1))
                if c >= 1 and c + 2 < NCH:
                    mask_pe(tensor, c + 2)
                for tt in range(NTT):
                    mm = c * NTT + tt
                    if mm >= 2:
                        tensor.wait_ge(s_ev, mm - 1)
                    for kt in range(NKT):
                        tensor.matmul(ps_o[mm % 2][:, 0:CW],
                                      xnT[:, kt * T + tt * 128:kt * T + tt * 128 + 128],
                                      wsum[pc][:, kt * CW:(kt + 1) * CW],
                                      start=(kt == 0), stop=False)
                    tensor.matmul(ps_o[mm % 2][:, 0:CW],
                                  h_sb[:, tt * 128:(tt + 1) * 128],
                                  cbt[pc][:], start=False, stop=False)
                    tensor.matmul(ps_o[mm % 2][:, 0:CW],
                                  ones_sb[:], bbf[pc][0:1, :],
                                  start=False, stop=True).then_inc(s_pmain, 1)
                    if c == 0:
                        if tt == 8:
                            mask_pe(tensor, 2)
                        elif tt == 9:
                            mask_pe2(tensor, 2)
                    elif c + 2 < NCH and tt == 0:
                        mask_pe2(tensor, c + 2)

        @block.scalar
        def _(scalar):
            for t in range(NTT):
                nfill = t // 3 + 1 if t < 14 else 6
                scalar.wait_ge(s_x[xb(t)], 32 * nfill)
                if t >= 1:
                    scalar.wait_ge(s_sq, t)
                if t >= 2:
                    scalar.wait_ge(s_dvc, ln_idx[(t - 2, "mean")])
                scalar.activation(out=sq_scr[:], in_=x_bufs[xb(t)][:], func=AF.Identity,
                                  accum_out=s1[t % 2][:]).then_inc(s_id, 1)
                scalar.wait_ge(s_id, t + 1)
                if t >= 2:
                    scalar.wait_ge(s_dvc, ln_idx[(t - 2, "vareps")])
                scalar.activation(out=sq_scr[:], in_=x_bufs[xb(t)][:], func=AF.Square,
                                  accum_out=s2[t % 2][:]).then_inc(s_sq, 1)
                scalar.wait_ge(s_dvc, ln_idx[(t, "rr")])
                if t >= 2:
                    scalar.wait_ge(s_dvc, ln_idx[(t - 2, "rstd")])
                scalar.activation(out=r0[t % 2][:], in_=rr[t % 2][:], func=AF.Sqrt).then_inc(s_r0, 1)
                if t % 4 == 1 and t >= 5:
                    tc = (t - 5) // 4
                    if tc == 0:
                        scalar.wait_ge(s_cv, 1)
                    scalar.wait_ge(s_peh, tc + 1)
                    scalar.activation(out=h_sb[:, tc * 512:(tc + 1) * 512], in_=ps_h[tc % 2][:, 0:512],
                                      func=AF.Identity, scale=cvec[:]).then_inc(s_hrdy, 1)
            scalar.wait_ge(s_peh, 4)
            scalar.activation(out=h_sb[:, 3 * 512:4 * 512], in_=ps_h[1][:, 0:512],
                              func=AF.Identity, scale=cvec[:]).then_inc(s_hrdy, 1)
            def resid_dma(og):
                c2, q2 = og // 4, og % 4
                if og < 2:
                    scalar.wait_ge(s_peh, 4)
                scalar.dma_start(
                    out=resid_db[og % 2][:].rearrange("p (j n) -> p j n", j=4),
                    in_=x_h[q2 * 512:(q2 + 1) * 512, c2 * CW:(c2 + 1) * CW]
                        .rearrange("(j p) n -> p j n", p=128),
                ).then_inc(s_rsd[og % 2], 16)

            resid_dma(0)
            resid_dma(1)
            for og in range(4 * NCH):
                c, q = og // 4, og % 4
                scalar.wait_ge(s_ev, (og + 1) * 4)
                if og + 2 < 4 * NCH:
                    resid_dma(og + 2)
                scalar.dma_start(
                    out=out_h[q * 512:(q + 1) * 512, c * CW:(c + 1) * CW]
                        .rearrange("(j p) n -> p j n", p=128),
                    in_=out_db[og % 2][:].rearrange("p (j n) -> p j n", j=4),
                ).then_inc(s_out[og % 2], 16)
            scalar.wait_ge(s_out[0], 16 * 2 * NCH)
            scalar.wait_ge(s_out[1], 16 * 2 * NCH)

    return nc


def _consts():
    ident = np.eye(128, dtype=ml_dtypes.bfloat16)
    g32 = np.zeros((128, 4), dtype=ml_dtypes.bfloat16)
    rep4 = np.zeros((4, 128), dtype=ml_dtypes.bfloat16)
    for n in range(128):
        g32[n, n // 32] = 1
        rep4[n // 32, n] = 1
    ones_row = np.ones((1, 128), dtype=ml_dtypes.bfloat16)
    return ident, g32, rep4, ones_row


def kernel(L0_attn_data, L0_attn_weight, L0_attn_mask, L0_ffn_lora_A, L0_ffn_lora_B,
           L0_ffn_W, L0_ffn_bias, L0_ffn_lora_weights):
    x = np.asarray(L0_attn_data, dtype=np.float32)
    wgtT = np.ascontiguousarray(np.asarray(L0_attn_weight, dtype=np.float32).T)
    maskT = np.ascontiguousarray(np.asarray(L0_attn_mask, dtype=np.int32).T)
    WT = np.ascontiguousarray(np.asarray(L0_ffn_W, dtype=np.float32).T)
    A_T = np.ascontiguousarray(np.asarray(L0_ffn_lora_A, dtype=np.float32).reshape(128, D).T)
    B_t = np.ascontiguousarray(np.asarray(L0_ffn_lora_B, dtype=np.float32).transpose(0, 2, 1).reshape(128, D))
    bias = np.ascontiguousarray(np.asarray(L0_ffn_bias, dtype=np.float32))
    lw_rep = np.ascontiguousarray(
        np.repeat(np.asarray(L0_ffn_lora_weights, dtype=np.float32), 16).reshape(128, 1))
    ident, g32, rep4, ones_row = _consts()

    nc = build_nc()
    in_maps = []
    for c in range(8):
        xs = np.ascontiguousarray(x[:, 1024 * c:1024 * (c + 1), :].reshape(T, D))
        in_maps.append(dict(x=xs, wt=WT, wgt=wgtT, maskt=maskT, at=A_T, bt=B_t,
                            bias=bias, lwrep=lw_rep, ident=ident, g32=g32,
                            rep4=rep4, onesrow=ones_row))
    res = run_bass_kernel_spmd(nc, in_maps, core_ids=list(range(8)))
    out = np.empty((2, 8192, D), dtype=np.float32)
    for c in range(8):
        out[:, 1024 * c:1024 * (c + 1), :] = np.asarray(res.results[c]["out"]).reshape(2, 1024, D)
    return out


# version 29
# speedup vs baseline: 1.0621x; 1.0024x over previous
import numpy as np
import ml_dtypes
from contextlib import ExitStack

import concourse.bass as bass
import concourse.mybir as mybir
from concourse.bass_utils import run_bass_kernel_spmd

F32 = mybir.dt.float32
BF16 = mybir.dt.bfloat16
I32 = mybir.dt.int32
AF = mybir.ActivationFunctionType
ALU = mybir.AluOpType
AX = mybir.AxisListType

D = 4096
T = 2048
NKT = 32
NCH = 16
CW = 256
NTT = 16
EPS = 1e-5
LNOPS = 10


def build_nc(debug=False):
    nc = bass.Bass("TRN2", target_bir_lowering=False)

    x_h = nc.dram_tensor("x", [T, D], F32, kind="ExternalInput").ap()
    wt_h = nc.dram_tensor("wt", [D, D], F32, kind="ExternalInput").ap()
    wgt_h = nc.dram_tensor("wgt", [D, D], F32, kind="ExternalInput").ap()
    mask_h = nc.dram_tensor("maskt", [D, D], I32, kind="ExternalInput").ap()
    at_h = nc.dram_tensor("at", [D, 128], F32, kind="ExternalInput").ap()
    bt_h = nc.dram_tensor("bt", [128, D], F32, kind="ExternalInput").ap()
    bias_h = nc.dram_tensor("bias", [D], F32, kind="ExternalInput").ap()
    lw_h = nc.dram_tensor("lwrep", [128, 1], F32, kind="ExternalInput").ap()
    id_h = nc.dram_tensor("ident", [128, 128], BF16, kind="ExternalInput").ap()
    g32_h = nc.dram_tensor("g32", [128, 4], BF16, kind="ExternalInput").ap()
    rep4_h = nc.dram_tensor("rep4", [4, 128], BF16, kind="ExternalInput").ap()
    ones_h = nc.dram_tensor("onesrow", [1, 128], BF16, kind="ExternalInput").ap()
    out_h = nc.dram_tensor("out", [T, D], F32, kind="ExternalOutput").ap()
    if debug:
        dbg_xnT = nc.dram_tensor("dbg_xnT", [128, 2048], F32, kind="ExternalOutput").ap()
        dbg_emT = nc.dram_tensor("dbg_emT", [128, 256], F32, kind="ExternalOutput").ap()
        dbg_h = nc.dram_tensor("dbg_h", [128, T], F32, kind="ExternalOutput").ap()
        dbg_ws = nc.dram_tensor("dbg_ws", [128, NKT * CW], F32, kind="ExternalOutput").ap()
        dbg_cv = nc.dram_tensor("dbg_cv", [128, 1], F32, kind="ExternalOutput").ap()

    A = nc.alloc_sbuf_tensor_at
    off = nc.sbuf_base

    def at_(name, shape, dtype, nbytes=None, align=32):
        nonlocal off
        off = (off + align - 1) // align * align
        t = A(name, shape, dtype, offset=off)
        sz = nbytes
        if sz is None:
            n = 1
            for s in shape[1:]:
                n *= s
            sz = n * mybir.dt.size(dtype)
        off += sz
        return t

    xnT = at_("xnT", [128, NKT * T], BF16)
    h_sb = at_("h_sb", [128, T], BF16)
    off = (off + 31) // 32 * 32
    _mask_off = off
    at_("maskraw", [128, 8 * CW], I32)
    mask_db = [A("maskdb%d" % i, [128, 2 * CW], I32, offset=_mask_off + i * 2048)
               for i in range(4)]
    mask_dbp = [A("maskdbp%d" % i, [128, 4 * CW], I32, offset=_mask_off + i * 4096)
                for i in range(2)]
    i128 = at_("i128", [128, 128], BF16)
    g32 = at_("g32s", [128, 4], BF16)
    rep4 = at_("rep4s", [4, 128], BF16, nbytes=256)
    ones_sb = at_("ones_sb", [1, 128], BF16, nbytes=256)
    lw_sb = at_("lw_sb", [128, 1], F32)
    cvec = at_("cvec", [128, 1], F32)
    s1 = [at_("s1%d" % i, [128, 1], F32) for i in range(2)]
    s2 = [at_("s2%d" % i, [128, 1], F32) for i in range(2)]
    mean = [at_("mean%d" % i, [128, 1], F32) for i in range(2)]
    vareps = [at_("vareps%d" % i, [128, 1], F32) for i in range(2)]
    m2 = [at_("m2%d" % i, [128, 1], F32) for i in range(2)]
    veps = [at_("veps%d" % i, [128, 1], F32) for i in range(2)]
    rr = [at_("rr%d" % i, [128, 1], F32) for i in range(2)]
    r0 = [at_("r0%d" % i, [128, 1], F32) for i in range(2)]
    t1 = [at_("t1%d" % i, [128, 1], F32) for i in range(2)]
    t3 = [at_("t3%d" % i, [128, 1], F32) for i in range(2)]
    rstd = [at_("rstd%d" % i, [128, 1], F32) for i in range(2)]
    nmr = [at_("nmr%d" % i, [128, 1], F32) for i in range(2)]
    ln_idx = {}
    _nd0 = 0
    for _t in range(NTT):
        for _nm in ("mean", "vareps", "m2", "veps", "rr",
                    "t1", "t3", "rstd", "nmr", "act"):
            _nd0 += 1
            ln_idx[(_t, _nm)] = _nd0
    assert _nd0 == LNOPS * NTT
    colsum = [at_("colsum%d" % i, [128, 256], BF16) for i in range(2)]
    bm = [at_("bm%d" % i, [4, 256], BF16, nbytes=512) for i in range(2)]
    emT = [at_("emT%d" % i, [128, 256], BF16) for i in range(2)]
    cbt = [at_("cbt%d" % i, [128, CW], BF16) for i in range(2)]
    bbf = [at_("bbf%d" % i, [1, CW], BF16, nbytes=512) for i in range(2)]

    arena = (off + 255) // 256 * 256
    wsum = [A("wsum%d" % i, [128, NKT * CW], BF16, offset=arena + i * 16384) for i in range(2)]
    wgt_db = [A("wgtdb%d" % i, [128, 8 * CW], BF16, offset=arena + 32768 + i * 4096) for i in range(2)]
    resid_db = [A("resid%d" % i, [128, 4 * CW], F32, offset=arena + 40960 + i * 4096) for i in range(2)]
    out_db = [A("outdb%d" % i, [128, 4 * CW], F32, offset=arena + 49152 + i * 4096) for i in range(2)]
    main_end = arena + 57344
    x_bufs = [A("x_buf%d" % i, [128, D], BF16, offset=arena + 16384 + i * 8192)
              for i in range(3)]

    def xb(t):
        return t % 3 if t < 14 else t - 14
    at_bf = A("at_bf", [128, NKT * 128], BF16, offset=arena + 40960)
    sq_scr = A("sq_scr", [128, D], BF16, offset=arena + 49152)
    assert main_end <= nc.sbuf_top, (main_end, nc.sbuf_top)

    ps_o = [nc.alloc_psum_tensor("ps_o%d" % i, [128, 512], F32) for i in range(2)]
    ps_x = [nc.alloc_psum_tensor("ps_x%d" % i, [128, 1024], BF16) for i in range(2)]
    ps_h = [nc.alloc_psum_tensor("ps_h%d" % i, [128, 512], F32) for i in range(2)]
    ps_m = nc.alloc_psum_tensor("ps_m", [128, 512], F32)
    ps_e = nc.alloc_psum_tensor("ps_e", [128, 512], F32)

    ctx = ExitStack()

    def sem(name):
        return ctx.enter_context(nc.semaphore(name))

    s_cst = sem("s_cst")
    s_x = [sem("s_x0"), sem("s_x1"), sem("s_x2")]
    s_msk = [sem("s_msk%d" % i) for i in range(4)]
    s_mskp = [sem("s_mskp%d" % i) for i in range(2)]
    s_mred2 = sem("s_mred2")
    s_rsd = [sem("s_rsd0"), sem("s_rsd1")]
    s_at = sem("s_at")
    s_wgt = [sem("s_wgt0"), sem("s_wgt1")]
    s_wW = [[sem("s_wW%d_%d" % (i, g)) for g in range(4)] for i in range(2)]
    s_cb = [sem("s_cb0"), sem("s_cb1")]
    s_bb = [sem("s_bb0"), sem("s_bb1")]
    s_mred = sem("s_mred")
    s_pem = sem("s_pem")
    s_bm = sem("s_bm")
    s_emT = sem("s_emT")
    s_sq = sem("s_sq")
    s_id = sem("s_id")
    s_dvc = sem("s_dvc")
    s_r0 = sem("s_r0")

    s_peg = sem("s_peg")
    s_sev = sem("s_sev")
    s_peh = sem("s_peh")
    s_hrdy = sem("s_hrdy")
    s_cv = sem("s_cv")
    s_wm = sem("s_wm")
    s_add = sem("s_add")
    s_pmain = sem("s_pmain")
    s_ev = sem("s_ev")
    s_out = [sem("s_out0"), sem("s_out1")]
    s_dbg = sem("s_dbg") if debug else None

    def mask_dma(eng, c, qs=tuple(range(16))):
        for q in qs:
            m = c * 16 + q
            if m >= 4:
                eng.wait_ge(s_mred, m - 3)
            eng.dma_start(
                out=mask_db[m % 4][:].rearrange("p (a b) -> p a b", a=2, b=CW),
                in_=mask_h[:, c * CW:(c + 1) * CW]
                    .rearrange("(kt p) n -> p kt n", p=128)[:, q * 2:(q + 1) * 2, :],
            ).then_inc(s_msk[m % 4], 16)

    def mask_dma_main(eng, c, qs=tuple(range(8))):
        for q in qs:
            m8 = 8 * (c - 2) + q
            if m8 == 0:
                eng.wait_ge(s_mred, 30)
            elif m8 == 1:
                eng.wait_ge(s_mred, 32)
            else:
                eng.wait_ge(s_mred2, m8 - 1)
            eng.dma_start(
                out=mask_dbp[m8 % 2][:].rearrange("p (a b) -> p a b", a=4, b=CW),
                in_=mask_h[:, c * CW:(c + 1) * CW]
                    .rearrange("(kt p) n -> p kt n", p=128)[:, q * 4:(q + 1) * 4, :],
            ).then_inc(s_mskp[m8 % 2], 16)

    def mask_reduce(vector, c, q):
        m = c * 16 + q
        par = c % 2
        vector.wait_ge(s_msk[m % 4], 16 * (m // 4 + 1))
        with nc.allow_low_precision(reason="int mask sums <=32 exact in bf16"):
            vector.tensor_reduce(
                out=colsum[par][:, q * 16:(q + 1) * 16].rearrange("p (a b) -> p a b", a=2, b=8),
                in_=mask_db[m % 4][:].rearrange("p (a b c) -> p a b c", a=2, b=8, c=32),
                axis=AX.X, op=ALU.add,
            ).then_inc(s_mred, 1)

    def mask_reduce_main(vector, c, q):
        m8 = 8 * (c - 2) + q
        par = c % 2
        vector.wait_ge(s_mskp[m8 % 2], 16 * (m8 // 2 + 1))
        if q == 0:
            vector.wait_ge(s_pem, 2 * (c - 2) + 1)
        with nc.allow_low_precision(reason="int mask sums <=32 exact in bf16"):
            vector.tensor_reduce(
                out=colsum[par][:, q * 32:(q + 1) * 32].rearrange("p (a b) -> p a b", a=4, b=8),
                in_=mask_dbp[m8 % 2][:].rearrange("p (a b c) -> p a b c", a=4, b=8, c=32),
                axis=AX.X, op=ALU.add,
            ).then_inc(s_mred2, 1)

    def mask_pe(tensor, c):
        par = c % 2
        if c <= 1:
            tensor.wait_ge(s_mred, 16 * (c + 1))
        else:
            tensor.wait_ge(s_mred2, 8 * (c - 1))
        if c >= 2:
            tensor.wait_ge(s_bm, c - 1)
        tensor.matmul(ps_m[0:4, 0:256], g32[:], colsum[par][:], start=True, stop=True).then_inc(s_pem, 1)

    def mask_pe2(tensor, c):
        par = c % 2
        tensor.wait_ge(s_bm, c + 1)
        if c >= 2:
            tensor.wait_ge(s_emT, c - 1)
        tensor.matmul(ps_e[:, 0:256], rep4[:], bm[par][:], start=True, stop=True).then_inc(s_pem, 1)

    def mask_min(vector, c):
        par = c % 2
        vector.wait_ge(s_pem, 2 * c + 1)
        with nc.allow_low_precision(reason="0/1 threshold exact in bf16"):
            vector.tensor_scalar_min(out=bm[par][:], in0=ps_m[0:4, 0:256], scalar1=1.0).then_inc(s_bm, 1)

    def mask_evac(vector, c):
        par = c % 2
        vector.wait_ge(s_pem, 2 * c + 2)
        with nc.allow_low_precision(reason="0/1 mask exact in bf16"):
            vector.tensor_copy(out=emT[par][:], in_=ps_e[:, 0:256]).then_inc(s_emT, 1)

    def wm_op(vector, c, g):
        gi = 4 * c + g
        vector.wait_ge(s_wgt[gi % 2], 16 * (gi // 2 + 1))
        vector.wait_ge(s_emT, c + 1)
        wg = wgt_db[gi % 2][:].rearrange("p (a b c) -> p a b c", a=8, b=8, c=32)
        emb = emT[c % 2][:, g * 64:(g + 1) * 64] \
            .rearrange("p (a b) -> p a b", a=8, b=8).unsqueeze(3).broadcast_to((128, 8, 8, 32))
        vector.tensor_tensor(out=wg, in0=wg, in1=emb, op=ALU.mult).then_inc(s_wm, 1)

    def add_op(vector, c, g):
        pc = c % 2
        vector.wait_ge(s_wm, c * 4 + g + 1)
        vector.wait_ge(s_wW[pc][g], 16 * (c // 2 + 1))
        vector.tensor_tensor(
            out=wsum[pc][:, g * 8 * CW:(g + 1) * 8 * CW],
            in0=wsum[pc][:, g * 8 * CW:(g + 1) * 8 * CW],
            in1=wgt_db[(4 * c + g) % 2][:],
            op=ALU.add,
        ).then_inc(s_add, 1)

    with nc.Block() as block:

        @block.sync
        def _(sync):
            sync.dma_start(out=i128[:], in_=id_h).then_inc(s_cst, 16)
            sync.dma_start(out=g32[:], in_=g32_h).then_inc(s_cst, 16)
            sync.dma_start(out=rep4[:], in_=rep4_h).then_inc(s_cst, 16)
            sync.dma_start(out=ones_sb[:], in_=ones_h).then_inc(s_cst, 16)
            sync.dma_start(out=lw_sb[:], in_=lw_h).then_inc(s_cst, 16)
            mask_dma(sync, 0)
            mask_dma(sync, 1)
            mask_dma_main(sync, 2)
            for c in range(NCH):
                if c + 3 < NCH:
                    mask_dma_main(sync, c + 3)

        @block.gpsimd
        def _(gpsimd):
            def wgt_dma(c, g):
                gi = 4 * c + g
                if c == 0 and g == 0:
                    gpsimd.wait_ge(s_dvc, ln_idx[(11, "act")])
                    gpsimd.wait_ge(s_sq, 12)
                    gpsimd.wait_ge(s_peg, 4 * 12)
                if gi >= 2:
                    gpsimd.wait_ge(s_add, gi - 1)
                gpsimd.dma_start(
                    out=wgt_db[gi % 2][:].rearrange("p (a b) -> p a b", a=8, b=CW),
                    in_=wgt_h[:, c * CW:(c + 1) * CW]
                        .rearrange("(kt p) n -> p kt n", p=128)[:, g * 8:(g + 1) * 8, :],
                ).then_inc(s_wgt[gi % 2], 16)

            def w_dma(c, g):
                pc = c % 2
                if c == 1:
                    gpsimd.wait_ge(s_dvc, ln_idx[(NTT - 1, "act")])
                    gpsimd.wait_ge(s_peg, 4 * NTT)
                elif c >= 2:
                    gpsimd.wait_ge(s_pmain, (c - 1) * NTT)
                gpsimd.dma_start(
                    out=wsum[pc][:, g * 8 * CW:(g + 1) * 8 * CW]
                        .rearrange("p (a b) -> p a b", a=8, b=CW),
                    in_=wt_h[:, c * CW:(c + 1) * CW]
                        .rearrange("(kt p) n -> p kt n", p=128)[:, g * 8:(g + 1) * 8, :],
                ).then_inc(s_wW[pc][g], 16)

            for t in range(NTT):
                tp = t - 3 if t < 14 else t - 2
                if t >= 3:
                    gpsimd.wait_ge(s_dvc, ln_idx[(tp, "act")])
                    gpsimd.wait_ge(s_sq, tp + 1)
                    gpsimd.wait_ge(s_peg, 4 * tp + 2)
                gpsimd.dma_start(out=x_bufs[xb(t)][:, 0:2048],
                                 in_=x_h[t * 128:(t + 1) * 128, 0:2048]).then_inc(s_x[xb(t)], 16)
                if t >= 3:
                    gpsimd.wait_ge(s_peg, 4 * tp + 4)
                gpsimd.dma_start(out=x_bufs[xb(t)][:, 2048:4096],
                                 in_=x_h[t * 128:(t + 1) * 128, 2048:4096]).then_inc(s_x[xb(t)], 16)
                if t == 0:
                    for i in range(2):
                        gpsimd.dma_start(
                            out=at_bf[:, i * 2048:(i + 1) * 2048].rearrange("p (a b) -> p a b", a=16, b=128),
                            in_=at_h.rearrange("(kt p) a -> p kt a", p=128)[:, i * 16:(i + 1) * 16, :],
                        ).then_inc(s_at, 16)
                elif 2 <= t <= 5:
                    w_dma(0, t - 2)
                elif t == 6:
                    gpsimd.dma_start(out=cbt[0][:], in_=bt_h[:, 0:CW]).then_inc(s_cb[0], 16)
                    gpsimd.dma_start(out=bbf[0][0:1, :],
                                     in_=bias_h[0:CW].unsqueeze(0)).then_inc(s_bb[0], 16)
            for g in range(4):
                wgt_dma(0, g)

            for c in range(1, NCH):
                pc = c % 2
                wgt_dma(c, 0)
                wgt_dma(c, 1)
                for g in range(4):
                    w_dma(c, g)
                if c >= 2:
                    gpsimd.wait_ge(s_pmain, (c - 1) * NTT)
                gpsimd.dma_start(out=cbt[pc][:], in_=bt_h[:, c * CW:(c + 1) * CW]).then_inc(s_cb[pc], 16)
                gpsimd.dma_start(out=bbf[pc][0:1, :],
                                 in_=bias_h[c * CW:(c + 1) * CW].unsqueeze(0)).then_inc(s_bb[pc], 16)
                wgt_dma(c, 2)
                wgt_dma(c, 3)
                if debug and c == 1:
                    gpsimd.wait_ge(s_sev, 4 * NTT)
                    gpsimd.dma_start(out=dbg_xnT, in_=xnT[:, 0:2048]).then_inc(s_dbg, 16)
                    gpsimd.wait_ge(s_emT, 1)
                    gpsimd.dma_start(out=dbg_emT, in_=emT[0][:]).then_inc(s_dbg, 16)
                    gpsimd.wait_ge(s_hrdy, 4)
                    gpsimd.dma_start(out=dbg_h, in_=h_sb[:]).then_inc(s_dbg, 16)
                    gpsimd.wait_ge(s_add, 4)
                    gpsimd.dma_start(out=dbg_ws, in_=wsum[0][:]).then_inc(s_dbg, 16)
                    gpsimd.wait_ge(s_cv, 1)
                    gpsimd.dma_start(out=dbg_cv, in_=cvec[:]).then_inc(s_dbg, 16)
                    gpsimd.wait_ge(s_dbg, 80)

        @block.vector
        def _(vector):
            nd = 0

            vector.wait_ge(s_cst, 5 * 16)
            vector.tensor_scalar_mul(out=cvec[:], in0=lw_sb[:], scalar1=2.0).then_inc(s_cv, 1)

            def dv(ins, tile=None, name=None):
                nonlocal nd
                if nd:
                    vector.wait_ge(s_dvc, nd)
                r = ins()
                nd += 1
                r.then_inc(s_dvc, 1)
                if name is not None:
                    assert ln_idx[(tile, name)] == nd, (tile, name, nd)

            def chain_a(t):
                p = t % 2
                vector.wait_ge(s_id, t + 1)
                dv(lambda: vector.tensor_scalar_mul(out=mean[p][:], in0=s1[p][:], scalar1=1.0 / D),
                   t, "mean")
                vector.wait_ge(s_sq, t + 1)
                dv(lambda: vector.tensor_scalar(out=vareps[p][:], in0=s2[p][:], scalar1=1.0 / D,
                                                scalar2=EPS, op0=ALU.mult, op1=ALU.add),
                   t, "vareps")
                dv(lambda: vector.tensor_mul(out=m2[p][:], in0=mean[p][:], in1=mean[p][:]))
                dv(lambda: vector.tensor_sub(out=veps[p][:], in0=vareps[p][:], in1=m2[p][:]))
                dv(lambda: vector.reciprocal(out=rr[p][:], in_=veps[p][:]), t, "rr")

            def chain_b(t):
                p = t % 2
                vector.wait_ge(s_r0, t + 1)
                dv(lambda: vector.scalar_tensor_tensor(out=t1[p][:], in0=r0[p][:], scalar=r0[p][:],
                                                       in1=veps[p][:], op0=ALU.mult, op1=ALU.mult))
                dv(lambda: vector.tensor_scalar(out=t3[p][:], in0=t1[p][:], scalar1=-0.5,
                                                scalar2=1.5, op0=ALU.mult, op1=ALU.add))
                dv(lambda: vector.tensor_mul(out=rstd[p][:], in0=r0[p][:], in1=t3[p][:]), t, "rstd")
                dv(lambda: vector.scalar_tensor_tensor(out=nmr[p][:], in0=mean[p][:], scalar=-1.0,
                                                       in1=rstd[p][:], op0=ALU.mult, op1=ALU.mult))
                with nc.allow_low_precision(reason="xn is a bf16 matmul operand"):
                    dv(lambda: vector.tensor_scalar(out=x_bufs[xb(t)][:], in0=x_bufs[xb(t)][:],
                                                    scalar1=rstd[p][:], scalar2=nmr[p][:],
                                                    op0=ALU.mult, op1=ALU.add), t, "act")

            for t in range(NTT + 1):
                if t >= 1:
                    for sg in range(4):
                        g = 4 * (t - 1) + sg
                        vector.wait_ge(s_peg, g + 1)
                        vector.tensor_copy(
                            out=xnT[:].rearrange("p (kt tt) -> p kt tt", kt=NKT)
                                [:, 8 * sg:8 * sg + 8, (t - 1) * 128:t * 128],
                            in_=ps_x[g % 2][:].rearrange("p (a b) -> p a b", a=8, b=128),
                        ).then_inc(s_sev, 1)
                if t < NTT:
                    chain_a(t)
                    chain_b(t)
                if t <= 15:
                    mask_reduce(vector, (2 * t) // 16, (2 * t) % 16)
                    mask_reduce(vector, (2 * t + 1) // 16, (2 * t + 1) % 16)
                if t == 9:
                    mask_min(vector, 0)
                elif t == 10:
                    mask_evac(vector, 0)
                elif t == 15:
                    wm_op(vector, 0, 0)
                elif t == 16:
                    mask_min(vector, 1)
                    mask_evac(vector, 1)
                    add_op(vector, 0, 0)
                    wm_op(vector, 0, 1)
            assert nd == LNOPS * NTT, nd
            add_op(vector, 0, 1)
            for g in range(2, 4):
                wm_op(vector, 0, g)
                add_op(vector, 0, g)

            for c in range(NCH):
                for tt in range(NTT):
                    mm = c * NTT + tt
                    og = c * 4 + tt // 4
                    if c == 0:
                        if tt < 8:
                            mask_reduce_main(vector, 2, tt)
                        elif tt == 9:
                            mask_min(vector, 2)
                        elif tt == 10:
                            mask_evac(vector, 2)
                    elif c + 2 < NCH:
                        if tt == 0:
                            mask_min(vector, c + 2)
                        elif tt == 1:
                            mask_evac(vector, c + 2)
                    vector.wait_ge(s_pmain, mm + 1)
                    if og < 2:
                        vector.wait_ge(s_sq, NTT)
                    vector.wait_ge(s_rsd[og % 2], 16 * (og // 2 + 1))
                    if og >= 2 and tt % 4 == 0:
                        vector.wait_ge(s_out[og % 2], 16 * (og // 2))
                    vector.scalar_tensor_tensor(
                        out=out_db[og % 2][:, (tt % 4) * CW:(tt % 4 + 1) * CW],
                        in0=ps_o[mm % 2][:, 0:CW], scalar=1.0,
                        in1=resid_db[og % 2][:, (tt % 4) * CW:(tt % 4 + 1) * CW],
                        op0=ALU.mult, op1=ALU.add,
                    ).then_inc(s_ev, 1)
                    if c + 1 < NCH and 2 <= tt < 10:
                        g = (tt - 2) // 2
                        if tt % 2 == 0:
                            wm_op(vector, c + 1, g)
                        else:
                            add_op(vector, c + 1, g)
                    if c + 3 < NCH and 7 <= tt < 15:
                        mask_reduce_main(vector, c + 3, tt - 7)

        @block.tensor
        def _(tensor):
            tensor.wait_ge(s_cst, 5 * 16)
            for t in range(NTT):
                tensor.wait_ge(s_dvc, ln_idx[(t, "act")])
                for sg in range(4):
                    g = 4 * t + sg
                    if g >= 2:
                        tensor.wait_ge(s_sev, g - 1)
                    for j in range(8):
                        ins = tensor.transpose(ps_x[g % 2][:, j * 128:(j + 1) * 128],
                                               x_bufs[xb(t)][:, (8 * sg + j) * 128:(8 * sg + j + 1) * 128],
                                               i128[:])
                        if j == 7:
                            ins.then_inc(s_peg, 1)
                if t == 15:
                    mask_pe(tensor, 1)
                if t % 4 == 3:
                    tc = t // 4
                    tensor.wait_ge(s_sev, 4 * (t + 1))
                    tensor.wait_ge(s_at, 32)
                    if tc >= 2:
                        tensor.wait_ge(s_hrdy, tc - 1)
                    for kt in range(NKT):
                        ins = tensor.matmul(ps_h[tc % 2][:, 0:512],
                                            at_bf[:, kt * 128:(kt + 1) * 128],
                                            xnT[:, kt * T + tc * 512:kt * T + tc * 512 + 512],
                                            start=(kt == 0), stop=(kt == NKT - 1))
                        if kt == NKT - 1:
                            ins.then_inc(s_peh, 1)
                if t == 8:
                    mask_pe(tensor, 0)
                elif t == 9:
                    mask_pe2(tensor, 0)
                elif t == 15:
                    mask_pe2(tensor, 1)
            tensor.wait_ge(s_hrdy, 4)
            for c in range(NCH):
                pc = c % 2
                tensor.wait_ge(s_add, (c + 1) * 4)
                tensor.wait_ge(s_cb[pc], 16 * (c // 2 + 1))
                tensor.wait_ge(s_bb[pc], 16 * (c // 2 + 1))
                if c >= 1 and c + 2 < NCH:
                    mask_pe(tensor, c + 2)
                for tt in range(NTT):
                    mm = c * NTT + tt
                    if mm >= 2:
                        tensor.wait_ge(s_ev, mm - 1)
                    for kt in range(NKT):
                        tensor.matmul(ps_o[mm % 2][:, 0:CW],
                                      xnT[:, kt * T + tt * 128:kt * T + tt * 128 + 128],
                                      wsum[pc][:, kt * CW:(kt + 1) * CW],
                                      start=(kt == 0), stop=False)
                    tensor.matmul(ps_o[mm % 2][:, 0:CW],
                                  h_sb[:, tt * 128:(tt + 1) * 128],
                                  cbt[pc][:], start=False, stop=False)
                    tensor.matmul(ps_o[mm % 2][:, 0:CW],
                                  ones_sb[:], bbf[pc][0:1, :],
                                  start=False, stop=True).then_inc(s_pmain, 1)
                    if c == 0:
                        if tt == 8:
                            mask_pe(tensor, 2)
                        elif tt == 9:
                            mask_pe2(tensor, 2)
                    elif c + 2 < NCH and tt == 0:
                        mask_pe2(tensor, c + 2)

        @block.scalar
        def _(scalar):
            for t in range(NTT):
                nfill = t // 3 + 1 if t < 14 else 6
                scalar.wait_ge(s_x[xb(t)], 32 * nfill)
                if t >= 1:
                    scalar.wait_ge(s_sq, t)
                if t >= 2:
                    scalar.wait_ge(s_dvc, ln_idx[(t - 2, "mean")])
                scalar.activation(out=sq_scr[:], in_=x_bufs[xb(t)][:], func=AF.Identity,
                                  accum_out=s1[t % 2][:]).then_inc(s_id, 1)
                scalar.wait_ge(s_id, t + 1)
                if t >= 2:
                    scalar.wait_ge(s_dvc, ln_idx[(t - 2, "vareps")])
                scalar.activation(out=sq_scr[:], in_=x_bufs[xb(t)][:], func=AF.Square,
                                  accum_out=s2[t % 2][:]).then_inc(s_sq, 1)
                scalar.wait_ge(s_dvc, ln_idx[(t, "rr")])
                if t >= 2:
                    scalar.wait_ge(s_dvc, ln_idx[(t - 2, "rstd")])
                scalar.activation(out=r0[t % 2][:], in_=rr[t % 2][:], func=AF.Sqrt).then_inc(s_r0, 1)
                if t % 4 == 1 and t >= 5:
                    tc = (t - 5) // 4
                    if tc == 0:
                        scalar.wait_ge(s_cv, 1)
                    scalar.wait_ge(s_peh, tc + 1)
                    scalar.activation(out=h_sb[:, tc * 512:(tc + 1) * 512], in_=ps_h[tc % 2][:, 0:512],
                                      func=AF.Identity, scale=cvec[:]).then_inc(s_hrdy, 1)
            scalar.wait_ge(s_peh, 4)
            scalar.activation(out=h_sb[:, 3 * 512:4 * 512], in_=ps_h[1][:, 0:512],
                              func=AF.Identity, scale=cvec[:]).then_inc(s_hrdy, 1)
            def resid_dma(og):
                c2, q2 = og // 4, og % 4
                if og < 2:
                    scalar.wait_ge(s_peh, 4)
                scalar.dma_start(
                    out=resid_db[og % 2][:].rearrange("p (j n) -> p j n", j=4),
                    in_=x_h[q2 * 512:(q2 + 1) * 512, c2 * CW:(c2 + 1) * CW]
                        .rearrange("(j p) n -> p j n", p=128),
                ).then_inc(s_rsd[og % 2], 16)

            resid_dma(0)
            resid_dma(1)
            for og in range(4 * NCH):
                c, q = og // 4, og % 4
                scalar.wait_ge(s_ev, (og + 1) * 4)
                if og + 2 < 4 * NCH:
                    resid_dma(og + 2)
                scalar.dma_start(
                    out=out_h[q * 512:(q + 1) * 512, c * CW:(c + 1) * CW]
                        .rearrange("(j p) n -> p j n", p=128),
                    in_=out_db[og % 2][:].rearrange("p (j n) -> p j n", j=4),
                ).then_inc(s_out[og % 2], 16)
            scalar.wait_ge(s_out[0], 16 * 2 * NCH)
            scalar.wait_ge(s_out[1], 16 * 2 * NCH)

    return nc


def _consts():
    ident = np.eye(128, dtype=ml_dtypes.bfloat16)
    g32 = np.zeros((128, 4), dtype=ml_dtypes.bfloat16)
    rep4 = np.zeros((4, 128), dtype=ml_dtypes.bfloat16)
    for n in range(128):
        g32[n, n // 32] = 1
        rep4[n // 32, n] = 1
    ones_row = np.ones((1, 128), dtype=ml_dtypes.bfloat16)
    return ident, g32, rep4, ones_row


def kernel(L0_attn_data, L0_attn_weight, L0_attn_mask, L0_ffn_lora_A, L0_ffn_lora_B,
           L0_ffn_W, L0_ffn_bias, L0_ffn_lora_weights):
    x = np.asarray(L0_attn_data, dtype=np.float32)
    wgtT = np.ascontiguousarray(np.asarray(L0_attn_weight, dtype=np.float32).T)
    maskT = np.ascontiguousarray(np.asarray(L0_attn_mask, dtype=np.int32).T)
    WT = np.ascontiguousarray(np.asarray(L0_ffn_W, dtype=np.float32).T)
    A_T = np.ascontiguousarray(np.asarray(L0_ffn_lora_A, dtype=np.float32).reshape(128, D).T)
    B_t = np.ascontiguousarray(np.asarray(L0_ffn_lora_B, dtype=np.float32).transpose(0, 2, 1).reshape(128, D))
    bias = np.ascontiguousarray(np.asarray(L0_ffn_bias, dtype=np.float32))
    lw_rep = np.ascontiguousarray(
        np.repeat(np.asarray(L0_ffn_lora_weights, dtype=np.float32), 16).reshape(128, 1))
    ident, g32, rep4, ones_row = _consts()

    nc = build_nc()
    in_maps = []
    for c in range(8):
        xs = np.ascontiguousarray(x[:, 1024 * c:1024 * (c + 1), :].reshape(T, D))
        in_maps.append(dict(x=xs, wt=WT, wgt=wgtT, maskt=maskT, at=A_T, bt=B_t,
                            bias=bias, lwrep=lw_rep, ident=ident, g32=g32,
                            rep4=rep4, onesrow=ones_row))
    res = run_bass_kernel_spmd(nc, in_maps, core_ids=list(range(8)))
    out = np.empty((2, 8192, D), dtype=np.float32)
    for c in range(8):
        out[:, 1024 * c:1024 * (c + 1), :] = np.asarray(res.results[c]["out"]).reshape(2, 1024, D)
    return out


# version 30
# speedup vs baseline: 1.0715x; 1.0089x over previous
import numpy as np
import ml_dtypes
from contextlib import ExitStack

import concourse.bass as bass
import concourse.mybir as mybir
from concourse.bass_utils import run_bass_kernel_spmd

F32 = mybir.dt.float32
BF16 = mybir.dt.bfloat16
I32 = mybir.dt.int32
AF = mybir.ActivationFunctionType
ALU = mybir.AluOpType
AX = mybir.AxisListType

D = 4096
T = 2048
NKT = 32
NCH = 16
CW = 256
NTT = 16
EPS = 1e-5
LNOPS = 8


def build_nc(debug=False):
    nc = bass.Bass("TRN2", target_bir_lowering=False)

    x_h = nc.dram_tensor("x", [T, D], F32, kind="ExternalInput").ap()
    wt_h = nc.dram_tensor("wt", [D, D], F32, kind="ExternalInput").ap()
    wgt_h = nc.dram_tensor("wgt", [D, D], F32, kind="ExternalInput").ap()
    mask_h = nc.dram_tensor("maskt", [D, D], I32, kind="ExternalInput").ap()
    at_h = nc.dram_tensor("at", [D, 128], F32, kind="ExternalInput").ap()
    bt_h = nc.dram_tensor("bt", [128, D], F32, kind="ExternalInput").ap()
    bias_h = nc.dram_tensor("bias", [D], F32, kind="ExternalInput").ap()
    lw_h = nc.dram_tensor("lwrep", [128, 1], F32, kind="ExternalInput").ap()
    id_h = nc.dram_tensor("ident", [128, 128], BF16, kind="ExternalInput").ap()
    g32_h = nc.dram_tensor("g32", [128, 4], BF16, kind="ExternalInput").ap()
    rep4_h = nc.dram_tensor("rep4", [4, 128], BF16, kind="ExternalInput").ap()
    ones_h = nc.dram_tensor("onesrow", [1, 128], BF16, kind="ExternalInput").ap()
    out_h = nc.dram_tensor("out", [T, D], F32, kind="ExternalOutput").ap()
    if debug:
        dbg_xnT = nc.dram_tensor("dbg_xnT", [128, 2048], F32, kind="ExternalOutput").ap()
        dbg_emT = nc.dram_tensor("dbg_emT", [128, 256], F32, kind="ExternalOutput").ap()
        dbg_h = nc.dram_tensor("dbg_h", [128, T], F32, kind="ExternalOutput").ap()
        dbg_ws = nc.dram_tensor("dbg_ws", [128, NKT * CW], F32, kind="ExternalOutput").ap()
        dbg_cv = nc.dram_tensor("dbg_cv", [128, 1], F32, kind="ExternalOutput").ap()

    A = nc.alloc_sbuf_tensor_at
    off = nc.sbuf_base

    def at_(name, shape, dtype, nbytes=None, align=32):
        nonlocal off
        off = (off + align - 1) // align * align
        t = A(name, shape, dtype, offset=off)
        sz = nbytes
        if sz is None:
            n = 1
            for s in shape[1:]:
                n *= s
            sz = n * mybir.dt.size(dtype)
        off += sz
        return t

    xnT = at_("xnT", [128, NKT * T], BF16)
    h_sb = at_("h_sb", [128, T], BF16)
    off = (off + 31) // 32 * 32
    _mask_off = off
    at_("maskraw", [128, 8 * CW], I32)
    mask_db = [A("maskdb%d" % i, [128, 2 * CW], I32, offset=_mask_off + i * 2048)
               for i in range(4)]
    mask_dbp = [A("maskdbp%d" % i, [128, 4 * CW], I32, offset=_mask_off + i * 4096)
                for i in range(2)]
    i128 = at_("i128", [128, 128], BF16)
    g32 = at_("g32s", [128, 4], BF16)
    rep4 = at_("rep4s", [4, 128], BF16, nbytes=256)
    ones_sb = at_("ones_sb", [1, 128], BF16, nbytes=256)
    lw_sb = at_("lw_sb", [128, 1], F32)
    cvec = at_("cvec", [128, 1], F32)
    s1 = [at_("s1%d" % i, [128, 1], F32) for i in range(2)]
    s2 = [at_("s2%d" % i, [128, 1], F32) for i in range(2)]
    mean = [at_("mean%d" % i, [128, 1], F32) for i in range(2)]
    vareps = [at_("vareps%d" % i, [128, 1], F32) for i in range(2)]
    m2 = [at_("m2%d" % i, [128, 1], F32) for i in range(2)]
    veps = [at_("veps%d" % i, [128, 1], F32) for i in range(2)]
    rr = [at_("rr%d" % i, [128, 1], F32) for i in range(2)]
    r0 = [at_("r0%d" % i, [128, 1], F32) for i in range(2)]
    t1 = [at_("t1%d" % i, [128, 1], F32) for i in range(2)]
    t3 = [at_("t3%d" % i, [128, 1], F32) for i in range(2)]
    rstd = [at_("rstd%d" % i, [128, 1], F32) for i in range(2)]
    nmr = [at_("nmr%d" % i, [128, 1], F32) for i in range(2)]
    ln_idx = {}
    _nd0 = 0
    for _t in range(NTT):
        for _nm in ("mean", "vareps", "m2", "veps", "rr",
                    "nmr", "act0", "act"):
            _nd0 += 1
            ln_idx[(_t, _nm)] = _nd0
    assert _nd0 == LNOPS * NTT
    colsum = [at_("colsum%d" % i, [128, 256], BF16) for i in range(2)]
    bm = [at_("bm%d" % i, [4, 256], BF16, nbytes=512) for i in range(2)]
    emT = [at_("emT%d" % i, [128, 256], BF16) for i in range(2)]
    cbt = [at_("cbt%d" % i, [128, CW], BF16) for i in range(2)]
    bbf = [at_("bbf%d" % i, [1, CW], BF16, nbytes=512) for i in range(2)]

    arena = (off + 255) // 256 * 256
    wsum = [A("wsum%d" % i, [128, NKT * CW], BF16, offset=arena + i * 16384) for i in range(2)]
    wgt_db = [A("wgtdb%d" % i, [128, 8 * CW], BF16, offset=arena + 32768 + i * 4096) for i in range(2)]
    resid_db = [A("resid%d" % i, [128, 4 * CW], F32, offset=arena + 40960 + i * 4096) for i in range(2)]
    out_db = [A("outdb%d" % i, [128, 4 * CW], F32, offset=arena + 49152 + i * 4096) for i in range(2)]
    main_end = arena + 57344
    x_bufs = [A("x_buf%d" % i, [128, D], BF16, offset=arena + 16384 + i * 8192)
              for i in range(3)]

    def xb(t):
        return t % 3 if t < 14 else t - 14
    at_bf = A("at_bf", [128, NKT * 128], BF16, offset=arena + 40960)
    sq_scr = A("sq_scr", [128, D], BF16, offset=arena + 49152)
    assert main_end <= nc.sbuf_top, (main_end, nc.sbuf_top)

    ps_o = [nc.alloc_psum_tensor("ps_o%d" % i, [128, 512], F32) for i in range(2)]
    ps_x = [nc.alloc_psum_tensor("ps_x%d" % i, [128, 1024], BF16) for i in range(2)]
    ps_h = [nc.alloc_psum_tensor("ps_h%d" % i, [128, 512], F32) for i in range(2)]
    ps_m = nc.alloc_psum_tensor("ps_m", [128, 512], F32)
    ps_e = nc.alloc_psum_tensor("ps_e", [128, 512], F32)

    ctx = ExitStack()

    def sem(name):
        return ctx.enter_context(nc.semaphore(name))

    s_cst = sem("s_cst")
    s_x = [sem("s_x0"), sem("s_x1"), sem("s_x2")]
    s_msk = [sem("s_msk%d" % i) for i in range(4)]
    s_mskp = [sem("s_mskp%d" % i) for i in range(2)]
    s_mred2 = sem("s_mred2")
    s_rsd = [sem("s_rsd0"), sem("s_rsd1")]
    s_at = sem("s_at")
    s_wgt = [sem("s_wgt0"), sem("s_wgt1")]
    s_wW = [[sem("s_wW%d_%d" % (i, g)) for g in range(4)] for i in range(2)]
    s_cb = [sem("s_cb0"), sem("s_cb1")]
    s_bb = [sem("s_bb0"), sem("s_bb1")]
    s_mred = sem("s_mred")
    s_pem = sem("s_pem")
    s_bm = sem("s_bm")
    s_emT = sem("s_emT")
    s_sq = sem("s_sq")
    s_id = sem("s_id")
    s_dvc = sem("s_dvc")
    s_r0 = sem("s_r0")

    s_peg = sem("s_peg")
    s_sev = sem("s_sev")
    s_peh = sem("s_peh")
    s_hrdy = sem("s_hrdy")
    s_cv = sem("s_cv")
    s_wm = sem("s_wm")
    s_add = sem("s_add")
    s_pmain = sem("s_pmain")
    s_ev = sem("s_ev")
    s_out = [sem("s_out0"), sem("s_out1")]
    s_dbg = sem("s_dbg") if debug else None

    def mask_dma(eng, c, qs=tuple(range(16))):
        for q in qs:
            m = c * 16 + q
            if m >= 4:
                eng.wait_ge(s_mred, m - 3)
            eng.dma_start(
                out=mask_db[m % 4][:].rearrange("p (a b) -> p a b", a=2, b=CW),
                in_=mask_h[:, c * CW:(c + 1) * CW]
                    .rearrange("(kt p) n -> p kt n", p=128)[:, q * 2:(q + 1) * 2, :],
            ).then_inc(s_msk[m % 4], 16)

    def mask_dma_main(eng, c, qs=tuple(range(8))):
        for q in qs:
            m8 = 8 * (c - 2) + q
            if m8 == 0:
                eng.wait_ge(s_mred, 30)
            elif m8 == 1:
                eng.wait_ge(s_mred, 32)
            else:
                eng.wait_ge(s_mred2, m8 - 1)
            eng.dma_start(
                out=mask_dbp[m8 % 2][:].rearrange("p (a b) -> p a b", a=4, b=CW),
                in_=mask_h[:, c * CW:(c + 1) * CW]
                    .rearrange("(kt p) n -> p kt n", p=128)[:, q * 4:(q + 1) * 4, :],
            ).then_inc(s_mskp[m8 % 2], 16)

    def mask_reduce(vector, c, q):
        m = c * 16 + q
        par = c % 2
        vector.wait_ge(s_msk[m % 4], 16 * (m // 4 + 1))
        with nc.allow_low_precision(reason="int mask sums <=32 exact in bf16"):
            vector.tensor_reduce(
                out=colsum[par][:, q * 16:(q + 1) * 16].rearrange("p (a b) -> p a b", a=2, b=8),
                in_=mask_db[m % 4][:].rearrange("p (a b c) -> p a b c", a=2, b=8, c=32),
                axis=AX.X, op=ALU.add,
            ).then_inc(s_mred, 1)

    def mask_reduce_main(vector, c, q):
        m8 = 8 * (c - 2) + q
        par = c % 2
        vector.wait_ge(s_mskp[m8 % 2], 16 * (m8 // 2 + 1))
        if q == 0:
            vector.wait_ge(s_pem, 2 * (c - 2) + 1)
        with nc.allow_low_precision(reason="int mask sums <=32 exact in bf16"):
            vector.tensor_reduce(
                out=colsum[par][:, q * 32:(q + 1) * 32].rearrange("p (a b) -> p a b", a=4, b=8),
                in_=mask_dbp[m8 % 2][:].rearrange("p (a b c) -> p a b c", a=4, b=8, c=32),
                axis=AX.X, op=ALU.add,
            ).then_inc(s_mred2, 1)

    def mask_pe(tensor, c):
        par = c % 2
        if c <= 1:
            tensor.wait_ge(s_mred, 16 * (c + 1))
        else:
            tensor.wait_ge(s_mred2, 8 * (c - 1))
        if c >= 2:
            tensor.wait_ge(s_bm, c - 1)
        tensor.matmul(ps_m[0:4, 0:256], g32[:], colsum[par][:], start=True, stop=True).then_inc(s_pem, 1)

    def mask_pe2(tensor, c):
        par = c % 2
        tensor.wait_ge(s_bm, c + 1)
        if c >= 2:
            tensor.wait_ge(s_emT, c - 1)
        tensor.matmul(ps_e[:, 0:256], rep4[:], bm[par][:], start=True, stop=True).then_inc(s_pem, 1)

    def mask_min(vector, c):
        par = c % 2
        vector.wait_ge(s_pem, 2 * c + 1)
        with nc.allow_low_precision(reason="0/1 threshold exact in bf16"):
            vector.tensor_scalar_min(out=bm[par][:], in0=ps_m[0:4, 0:256], scalar1=1.0).then_inc(s_bm, 1)

    def mask_evac(vector, c):
        par = c % 2
        vector.wait_ge(s_pem, 2 * c + 2)
        with nc.allow_low_precision(reason="0/1 mask exact in bf16"):
            vector.tensor_copy(out=emT[par][:], in_=ps_e[:, 0:256]).then_inc(s_emT, 1)

    def wm_op(vector, c, g):
        gi = 4 * c + g
        vector.wait_ge(s_wgt[gi % 2], 16 * (gi // 2 + 1))
        vector.wait_ge(s_emT, c + 1)
        wg = wgt_db[gi % 2][:].rearrange("p (a b c) -> p a b c", a=8, b=8, c=32)
        emb = emT[c % 2][:, g * 64:(g + 1) * 64] \
            .rearrange("p (a b) -> p a b", a=8, b=8).unsqueeze(3).broadcast_to((128, 8, 8, 32))
        vector.tensor_tensor(out=wg, in0=wg, in1=emb, op=ALU.mult).then_inc(s_wm, 1)

    def add_op(vector, c, g):
        pc = c % 2
        vector.wait_ge(s_wm, c * 4 + g + 1)
        vector.wait_ge(s_wW[pc][g], 16 * (c // 2 + 1))
        vector.tensor_tensor(
            out=wsum[pc][:, g * 8 * CW:(g + 1) * 8 * CW],
            in0=wsum[pc][:, g * 8 * CW:(g + 1) * 8 * CW],
            in1=wgt_db[(4 * c + g) % 2][:],
            op=ALU.add,
        ).then_inc(s_add, 1)

    with nc.Block() as block:

        @block.sync
        def _(sync):
            sync.dma_start(out=i128[:], in_=id_h).then_inc(s_cst, 16)
            sync.dma_start(out=g32[:], in_=g32_h).then_inc(s_cst, 16)
            sync.dma_start(out=rep4[:], in_=rep4_h).then_inc(s_cst, 16)
            sync.dma_start(out=ones_sb[:], in_=ones_h).then_inc(s_cst, 16)
            sync.dma_start(out=lw_sb[:], in_=lw_h).then_inc(s_cst, 16)
            mask_dma(sync, 0)
            mask_dma(sync, 1)
            mask_dma_main(sync, 2)
            for c in range(NCH):
                if c + 3 < NCH:
                    mask_dma_main(sync, c + 3)

        @block.gpsimd
        def _(gpsimd):
            def wgt_dma(c, g):
                gi = 4 * c + g
                if c == 0 and g == 0:
                    gpsimd.wait_ge(s_dvc, ln_idx[(11, "act")])
                    gpsimd.wait_ge(s_sq, 12)
                    gpsimd.wait_ge(s_peg, 4 * 12)
                if gi >= 2:
                    gpsimd.wait_ge(s_add, gi - 1)
                gpsimd.dma_start(
                    out=wgt_db[gi % 2][:].rearrange("p (a b) -> p a b", a=8, b=CW),
                    in_=wgt_h[:, c * CW:(c + 1) * CW]
                        .rearrange("(kt p) n -> p kt n", p=128)[:, g * 8:(g + 1) * 8, :],
                ).then_inc(s_wgt[gi % 2], 16)

            def w_dma(c, g):
                pc = c % 2
                if c == 1:
                    gpsimd.wait_ge(s_dvc, ln_idx[(NTT - 1, "act")])
                    gpsimd.wait_ge(s_peg, 4 * NTT)
                elif c >= 2:
                    gpsimd.wait_ge(s_pmain, (c - 1) * NTT)
                gpsimd.dma_start(
                    out=wsum[pc][:, g * 8 * CW:(g + 1) * 8 * CW]
                        .rearrange("p (a b) -> p a b", a=8, b=CW),
                    in_=wt_h[:, c * CW:(c + 1) * CW]
                        .rearrange("(kt p) n -> p kt n", p=128)[:, g * 8:(g + 1) * 8, :],
                ).then_inc(s_wW[pc][g], 16)

            for t in range(NTT):
                tp = t - 3 if t < 14 else t - 2
                if t >= 3:
                    gpsimd.wait_ge(s_dvc, ln_idx[(tp, "act")])
                    gpsimd.wait_ge(s_sq, tp + 1)
                    gpsimd.wait_ge(s_peg, 4 * tp + 2)
                gpsimd.dma_start(out=x_bufs[xb(t)][:, 0:2048],
                                 in_=x_h[t * 128:(t + 1) * 128, 0:2048]).then_inc(s_x[xb(t)], 16)
                if t >= 3:
                    gpsimd.wait_ge(s_peg, 4 * tp + 4)
                gpsimd.dma_start(out=x_bufs[xb(t)][:, 2048:4096],
                                 in_=x_h[t * 128:(t + 1) * 128, 2048:4096]).then_inc(s_x[xb(t)], 16)
                if t == 0:
                    for i in range(2):
                        gpsimd.dma_start(
                            out=at_bf[:, i * 2048:(i + 1) * 2048].rearrange("p (a b) -> p a b", a=16, b=128),
                            in_=at_h.rearrange("(kt p) a -> p kt a", p=128)[:, i * 16:(i + 1) * 16, :],
                        ).then_inc(s_at, 16)
                elif 2 <= t <= 5:
                    w_dma(0, t - 2)
                elif t == 6:
                    gpsimd.dma_start(out=cbt[0][:], in_=bt_h[:, 0:CW]).then_inc(s_cb[0], 16)
                    gpsimd.dma_start(out=bbf[0][0:1, :],
                                     in_=bias_h[0:CW].unsqueeze(0)).then_inc(s_bb[0], 16)
            for g in range(4):
                wgt_dma(0, g)

            for c in range(1, NCH):
                pc = c % 2
                wgt_dma(c, 0)
                wgt_dma(c, 1)
                for g in range(4):
                    w_dma(c, g)
                if c >= 2:
                    gpsimd.wait_ge(s_pmain, (c - 1) * NTT)
                gpsimd.dma_start(out=cbt[pc][:], in_=bt_h[:, c * CW:(c + 1) * CW]).then_inc(s_cb[pc], 16)
                gpsimd.dma_start(out=bbf[pc][0:1, :],
                                 in_=bias_h[c * CW:(c + 1) * CW].unsqueeze(0)).then_inc(s_bb[pc], 16)
                wgt_dma(c, 2)
                wgt_dma(c, 3)
                if debug and c == 1:
                    gpsimd.wait_ge(s_sev, 4 * NTT)
                    gpsimd.dma_start(out=dbg_xnT, in_=xnT[:, 0:2048]).then_inc(s_dbg, 16)
                    gpsimd.wait_ge(s_emT, 1)
                    gpsimd.dma_start(out=dbg_emT, in_=emT[0][:]).then_inc(s_dbg, 16)
                    gpsimd.wait_ge(s_hrdy, 4)
                    gpsimd.dma_start(out=dbg_h, in_=h_sb[:]).then_inc(s_dbg, 16)
                    gpsimd.wait_ge(s_add, 4)
                    gpsimd.dma_start(out=dbg_ws, in_=wsum[0][:]).then_inc(s_dbg, 16)
                    gpsimd.wait_ge(s_cv, 1)
                    gpsimd.dma_start(out=dbg_cv, in_=cvec[:]).then_inc(s_dbg, 16)
                    gpsimd.wait_ge(s_dbg, 80)

        @block.vector
        def _(vector):
            nd = 0

            vector.wait_ge(s_cst, 5 * 16)
            vector.tensor_scalar_mul(out=cvec[:], in0=lw_sb[:], scalar1=2.0).then_inc(s_cv, 1)

            def dv(ins, tile=None, name=None):
                nonlocal nd
                if nd:
                    vector.wait_ge(s_dvc, nd)
                r = ins()
                nd += 1
                r.then_inc(s_dvc, 1)
                if name is not None:
                    assert ln_idx[(tile, name)] == nd, (tile, name, nd)

            def chain_a(t):
                p = t % 2
                vector.wait_ge(s_id, t + 1)
                dv(lambda: vector.tensor_scalar_mul(out=mean[p][:], in0=s1[p][:], scalar1=1.0 / D),
                   t, "mean")
                vector.wait_ge(s_sq, t + 1)
                dv(lambda: vector.tensor_scalar(out=vareps[p][:], in0=s2[p][:], scalar1=1.0 / D,
                                                scalar2=EPS, op0=ALU.mult, op1=ALU.add),
                   t, "vareps")
                dv(lambda: vector.tensor_mul(out=m2[p][:], in0=mean[p][:], in1=mean[p][:]))
                dv(lambda: vector.tensor_sub(out=veps[p][:], in0=vareps[p][:], in1=m2[p][:]))
                dv(lambda: vector.reciprocal(out=rr[p][:], in_=veps[p][:]), t, "rr")

            def chain_b(t):
                p = t % 2
                vector.wait_ge(s_r0, t + 1)
                dv(lambda: vector.scalar_tensor_tensor(out=nmr[p][:], in0=mean[p][:], scalar=-1.0,
                                                       in1=r0[p][:], op0=ALU.mult, op1=ALU.mult),
                   t, "nmr")
                with nc.allow_low_precision(reason="xn is a bf16 matmul operand"):
                    dv(lambda: vector.tensor_scalar(out=x_bufs[xb(t)][:, 0:2048],
                                                    in0=x_bufs[xb(t)][:, 0:2048],
                                                    scalar1=r0[p][:], scalar2=nmr[p][:],
                                                    op0=ALU.mult, op1=ALU.add), t, "act0")
                    dv(lambda: vector.tensor_scalar(out=x_bufs[xb(t)][:, 2048:4096],
                                                    in0=x_bufs[xb(t)][:, 2048:4096],
                                                    scalar1=r0[p][:], scalar2=nmr[p][:],
                                                    op0=ALU.mult, op1=ALU.add), t, "act")

            for t in range(NTT + 1):
                if t >= 1:
                    for sg in range(4):
                        g = 4 * (t - 1) + sg
                        vector.wait_ge(s_peg, g + 1)
                        vector.tensor_copy(
                            out=xnT[:].rearrange("p (kt tt) -> p kt tt", kt=NKT)
                                [:, 8 * sg:8 * sg + 8, (t - 1) * 128:t * 128],
                            in_=ps_x[g % 2][:].rearrange("p (a b) -> p a b", a=8, b=128),
                        ).then_inc(s_sev, 1)
                if t < NTT:
                    chain_a(t)
                    chain_b(t)
                if t <= 15:
                    mask_reduce(vector, (2 * t) // 16, (2 * t) % 16)
                    mask_reduce(vector, (2 * t + 1) // 16, (2 * t + 1) % 16)
                if t == 9:
                    mask_min(vector, 0)
                elif t == 10:
                    mask_evac(vector, 0)
                elif t == 15:
                    wm_op(vector, 0, 0)
                elif t == 16:
                    mask_min(vector, 1)
                    mask_evac(vector, 1)
                    add_op(vector, 0, 0)
                    wm_op(vector, 0, 1)
            assert nd == LNOPS * NTT, nd
            add_op(vector, 0, 1)
            for g in range(2, 4):
                wm_op(vector, 0, g)
                add_op(vector, 0, g)

            for c in range(NCH):
                for tt in range(NTT):
                    mm = c * NTT + tt
                    og = c * 4 + tt // 4
                    if c == 0:
                        if tt < 8:
                            mask_reduce_main(vector, 2, tt)
                        elif tt == 9:
                            mask_min(vector, 2)
                        elif tt == 10:
                            mask_evac(vector, 2)
                    elif c + 2 < NCH:
                        if tt == 0:
                            mask_min(vector, c + 2)
                        elif tt == 1:
                            mask_evac(vector, c + 2)
                    vector.wait_ge(s_pmain, mm + 1)
                    if og < 2:
                        vector.wait_ge(s_sq, NTT)
                    vector.wait_ge(s_rsd[og % 2], 16 * (og // 2 + 1))
                    if og >= 2 and tt % 4 == 0:
                        vector.wait_ge(s_out[og % 2], 16 * (og // 2))
                    vector.scalar_tensor_tensor(
                        out=out_db[og % 2][:, (tt % 4) * CW:(tt % 4 + 1) * CW],
                        in0=ps_o[mm % 2][:, 0:CW], scalar=1.0,
                        in1=resid_db[og % 2][:, (tt % 4) * CW:(tt % 4 + 1) * CW],
                        op0=ALU.mult, op1=ALU.add,
                    ).then_inc(s_ev, 1)
                    if c + 1 < NCH and 2 <= tt < 10:
                        g = (tt - 2) // 2
                        if tt % 2 == 0:
                            wm_op(vector, c + 1, g)
                        else:
                            add_op(vector, c + 1, g)
                    if c + 3 < NCH and 7 <= tt < 15:
                        mask_reduce_main(vector, c + 3, tt - 7)

        @block.tensor
        def _(tensor):
            tensor.wait_ge(s_cst, 5 * 16)
            for t in range(NTT):
                tensor.wait_ge(s_dvc, ln_idx[(t, "act0")])
                for sg in range(4):
                    g = 4 * t + sg
                    if sg == 2:
                        tensor.wait_ge(s_dvc, ln_idx[(t, "act")])
                    if g >= 2:
                        tensor.wait_ge(s_sev, g - 1)
                    for j in range(8):
                        ins = tensor.transpose(ps_x[g % 2][:, j * 128:(j + 1) * 128],
                                               x_bufs[xb(t)][:, (8 * sg + j) * 128:(8 * sg + j + 1) * 128],
                                               i128[:])
                        if j == 7:
                            ins.then_inc(s_peg, 1)
                if t == 15:
                    mask_pe(tensor, 1)
                if t % 4 == 3:
                    tc = t // 4
                    tensor.wait_ge(s_sev, 4 * (t + 1))
                    tensor.wait_ge(s_at, 32)
                    if tc >= 2:
                        tensor.wait_ge(s_hrdy, tc - 1)
                    for kt in range(NKT):
                        ins = tensor.matmul(ps_h[tc % 2][:, 0:512],
                                            at_bf[:, kt * 128:(kt + 1) * 128],
                                            xnT[:, kt * T + tc * 512:kt * T + tc * 512 + 512],
                                            start=(kt == 0), stop=(kt == NKT - 1))
                        if kt == NKT - 1:
                            ins.then_inc(s_peh, 1)
                if t == 8:
                    mask_pe(tensor, 0)
                elif t == 9:
                    mask_pe2(tensor, 0)
                elif t == 15:
                    mask_pe2(tensor, 1)
            tensor.wait_ge(s_hrdy, 4)
            for c in range(NCH):
                pc = c % 2
                tensor.wait_ge(s_add, (c + 1) * 4)
                tensor.wait_ge(s_cb[pc], 16 * (c // 2 + 1))
                tensor.wait_ge(s_bb[pc], 16 * (c // 2 + 1))
                if c >= 1 and c + 2 < NCH:
                    mask_pe(tensor, c + 2)
                for tt in range(NTT):
                    mm = c * NTT + tt
                    if mm >= 2:
                        tensor.wait_ge(s_ev, mm - 1)
                    for kt in range(NKT):
                        tensor.matmul(ps_o[mm % 2][:, 0:CW],
                                      xnT[:, kt * T + tt * 128:kt * T + tt * 128 + 128],
                                      wsum[pc][:, kt * CW:(kt + 1) * CW],
                                      start=(kt == 0), stop=False)
                    tensor.matmul(ps_o[mm % 2][:, 0:CW],
                                  h_sb[:, tt * 128:(tt + 1) * 128],
                                  cbt[pc][:], start=False, stop=False)
                    tensor.matmul(ps_o[mm % 2][:, 0:CW],
                                  ones_sb[:], bbf[pc][0:1, :],
                                  start=False, stop=True).then_inc(s_pmain, 1)
                    if c == 0:
                        if tt == 8:
                            mask_pe(tensor, 2)
                        elif tt == 9:
                            mask_pe2(tensor, 2)
                    elif c + 2 < NCH and tt == 0:
                        mask_pe2(tensor, c + 2)

        @block.scalar
        def _(scalar):
            for t in range(NTT):
                nfill = t // 3 + 1 if t < 14 else 6
                scalar.wait_ge(s_x[xb(t)], 32 * nfill)
                if t >= 1:
                    scalar.wait_ge(s_sq, t)
                if t >= 2:
                    scalar.wait_ge(s_dvc, ln_idx[(t - 2, "mean")])
                scalar.activation(out=sq_scr[:], in_=x_bufs[xb(t)][:], func=AF.Identity,
                                  accum_out=s1[t % 2][:]).then_inc(s_id, 1)
                scalar.wait_ge(s_id, t + 1)
                if t >= 2:
                    scalar.wait_ge(s_dvc, ln_idx[(t - 2, "vareps")])
                scalar.activation(out=sq_scr[:], in_=x_bufs[xb(t)][:], func=AF.Square,
                                  accum_out=s2[t % 2][:]).then_inc(s_sq, 1)
                scalar.wait_ge(s_dvc, ln_idx[(t, "rr")])
                if t >= 2:
                    scalar.wait_ge(s_dvc, ln_idx[(t - 2, "act")])
                scalar.activation(out=r0[t % 2][:], in_=rr[t % 2][:], func=AF.Sqrt).then_inc(s_r0, 1)
                if t % 4 == 1 and t >= 5:
                    tc = (t - 5) // 4
                    if tc == 0:
                        scalar.wait_ge(s_cv, 1)
                    scalar.wait_ge(s_peh, tc + 1)
                    scalar.activation(out=h_sb[:, tc * 512:(tc + 1) * 512], in_=ps_h[tc % 2][:, 0:512],
                                      func=AF.Identity, scale=cvec[:]).then_inc(s_hrdy, 1)
            scalar.wait_ge(s_peh, 4)
            scalar.activation(out=h_sb[:, 3 * 512:4 * 512], in_=ps_h[1][:, 0:512],
                              func=AF.Identity, scale=cvec[:]).then_inc(s_hrdy, 1)
            def resid_dma(og):
                c2, q2 = og // 4, og % 4
                if og < 2:
                    scalar.wait_ge(s_peh, 4)
                scalar.dma_start(
                    out=resid_db[og % 2][:].rearrange("p (j n) -> p j n", j=4),
                    in_=x_h[q2 * 512:(q2 + 1) * 512, c2 * CW:(c2 + 1) * CW]
                        .rearrange("(j p) n -> p j n", p=128),
                ).then_inc(s_rsd[og % 2], 16)

            resid_dma(0)
            resid_dma(1)
            for og in range(4 * NCH):
                c, q = og // 4, og % 4
                scalar.wait_ge(s_ev, (og + 1) * 4)
                if og + 2 < 4 * NCH:
                    resid_dma(og + 2)
                scalar.dma_start(
                    out=out_h[q * 512:(q + 1) * 512, c * CW:(c + 1) * CW]
                        .rearrange("(j p) n -> p j n", p=128),
                    in_=out_db[og % 2][:].rearrange("p (j n) -> p j n", j=4),
                ).then_inc(s_out[og % 2], 16)
            scalar.wait_ge(s_out[0], 16 * 2 * NCH)
            scalar.wait_ge(s_out[1], 16 * 2 * NCH)

    return nc


def _consts():
    ident = np.eye(128, dtype=ml_dtypes.bfloat16)
    g32 = np.zeros((128, 4), dtype=ml_dtypes.bfloat16)
    rep4 = np.zeros((4, 128), dtype=ml_dtypes.bfloat16)
    for n in range(128):
        g32[n, n // 32] = 1
        rep4[n // 32, n] = 1
    ones_row = np.ones((1, 128), dtype=ml_dtypes.bfloat16)
    return ident, g32, rep4, ones_row


def kernel(L0_attn_data, L0_attn_weight, L0_attn_mask, L0_ffn_lora_A, L0_ffn_lora_B,
           L0_ffn_W, L0_ffn_bias, L0_ffn_lora_weights):
    x = np.asarray(L0_attn_data, dtype=np.float32)
    wgtT = np.ascontiguousarray(np.asarray(L0_attn_weight, dtype=np.float32).T)
    maskT = np.ascontiguousarray(np.asarray(L0_attn_mask, dtype=np.int32).T)
    WT = np.ascontiguousarray(np.asarray(L0_ffn_W, dtype=np.float32).T)
    A_T = np.ascontiguousarray(np.asarray(L0_ffn_lora_A, dtype=np.float32).reshape(128, D).T)
    B_t = np.ascontiguousarray(np.asarray(L0_ffn_lora_B, dtype=np.float32).transpose(0, 2, 1).reshape(128, D))
    bias = np.ascontiguousarray(np.asarray(L0_ffn_bias, dtype=np.float32))
    lw_rep = np.ascontiguousarray(
        np.repeat(np.asarray(L0_ffn_lora_weights, dtype=np.float32), 16).reshape(128, 1))
    ident, g32, rep4, ones_row = _consts()

    nc = build_nc()
    in_maps = []
    for c in range(8):
        xs = np.ascontiguousarray(x[:, 1024 * c:1024 * (c + 1), :].reshape(T, D))
        in_maps.append(dict(x=xs, wt=WT, wgt=wgtT, maskt=maskT, at=A_T, bt=B_t,
                            bias=bias, lwrep=lw_rep, ident=ident, g32=g32,
                            rep4=rep4, onesrow=ones_row))
    res = run_bass_kernel_spmd(nc, in_maps, core_ids=list(range(8)))
    out = np.empty((2, 8192, D), dtype=np.float32)
    for c in range(8):
        out[:, 1024 * c:1024 * (c + 1), :] = np.asarray(res.results[c]["out"]).reshape(2, 1024, D)
    return out


# version 32
# speedup vs baseline: 1.1128x; 1.0385x over previous
import numpy as np
import ml_dtypes
from contextlib import ExitStack

import concourse.bass as bass
import concourse.mybir as mybir
from concourse.bass_utils import run_bass_kernel_spmd

F32 = mybir.dt.float32
BF16 = mybir.dt.bfloat16
I32 = mybir.dt.int32
AF = mybir.ActivationFunctionType
ALU = mybir.AluOpType
AX = mybir.AxisListType

D = 4096
T = 2048
NKT = 32
NCH = 16
CW = 256
NTT = 16
EPS = 1e-5
LNOPS = 8


def build_nc(debug=False):
    nc = bass.Bass("TRN2", target_bir_lowering=False)

    x_h = nc.dram_tensor("x", [T, D], F32, kind="ExternalInput").ap()
    wt_h = nc.dram_tensor("wt", [D, D], F32, kind="ExternalInput").ap()
    wgt_h = nc.dram_tensor("wgt", [D, D], F32, kind="ExternalInput").ap()
    mask_h = nc.dram_tensor("maskt", [D, D], I32, kind="ExternalInput").ap()
    at_h = nc.dram_tensor("at", [D, 128], F32, kind="ExternalInput").ap()
    bt_h = nc.dram_tensor("bt", [128, D], F32, kind="ExternalInput").ap()
    bias_h = nc.dram_tensor("bias", [D], F32, kind="ExternalInput").ap()
    lw_h = nc.dram_tensor("lwrep", [128, 1], F32, kind="ExternalInput").ap()
    id_h = nc.dram_tensor("ident", [128, 128], BF16, kind="ExternalInput").ap()
    g32_h = nc.dram_tensor("g32", [128, 4], BF16, kind="ExternalInput").ap()
    rep4_h = nc.dram_tensor("rep4", [4, 128], BF16, kind="ExternalInput").ap()
    ones_h = nc.dram_tensor("onesrow", [1, 128], BF16, kind="ExternalInput").ap()
    out_h = nc.dram_tensor("out", [T, D], F32, kind="ExternalOutput").ap()
    if debug:
        dbg_xnT = nc.dram_tensor("dbg_xnT", [128, 2048], F32, kind="ExternalOutput").ap()
        dbg_emT = nc.dram_tensor("dbg_emT", [128, 256], F32, kind="ExternalOutput").ap()
        dbg_h = nc.dram_tensor("dbg_h", [128, T], F32, kind="ExternalOutput").ap()
        dbg_ws = nc.dram_tensor("dbg_ws", [128, NKT * CW], F32, kind="ExternalOutput").ap()
        dbg_cv = nc.dram_tensor("dbg_cv", [128, 1], F32, kind="ExternalOutput").ap()

    A = nc.alloc_sbuf_tensor_at
    off = nc.sbuf_base

    def at_(name, shape, dtype, nbytes=None, align=32):
        nonlocal off
        off = (off + align - 1) // align * align
        t = A(name, shape, dtype, offset=off)
        sz = nbytes
        if sz is None:
            n = 1
            for s in shape[1:]:
                n *= s
            sz = n * mybir.dt.size(dtype)
        off += sz
        return t

    xnT = at_("xnT", [128, NKT * T], BF16)
    h_sb = at_("h_sb", [128, T], BF16)
    off = (off + 31) // 32 * 32
    _mask_off = off
    at_("maskraw", [128, 8 * CW], I32)
    mask_db = [A("maskdb%d" % i, [128, 2 * CW], I32, offset=_mask_off + i * 2048)
               for i in range(4)]
    mask_dbp = [A("maskdbp%d" % i, [128, 4 * CW], I32, offset=_mask_off + i * 4096)
                for i in range(2)]
    i128 = at_("i128", [128, 128], BF16)
    g32 = at_("g32s", [128, 4], BF16)
    rep4 = at_("rep4s", [4, 128], BF16, nbytes=256)
    ones_sb = at_("ones_sb", [1, 128], BF16, nbytes=256)
    lw_sb = at_("lw_sb", [128, 1], F32)
    cvec = at_("cvec", [128, 1], F32)
    s1 = [at_("s1%d" % i, [128, 1], F32) for i in range(2)]
    s2 = [at_("s2%d" % i, [128, 1], F32) for i in range(2)]
    mean = [at_("mean%d" % i, [128, 1], F32) for i in range(2)]
    vareps = [at_("vareps%d" % i, [128, 1], F32) for i in range(2)]
    m2 = [at_("m2%d" % i, [128, 1], F32) for i in range(2)]
    veps = [at_("veps%d" % i, [128, 1], F32) for i in range(2)]
    rr = [at_("rr%d" % i, [128, 1], F32) for i in range(2)]
    r0 = [at_("r0%d" % i, [128, 1], F32) for i in range(2)]
    t1 = [at_("t1%d" % i, [128, 1], F32) for i in range(2)]
    t3 = [at_("t3%d" % i, [128, 1], F32) for i in range(2)]
    rstd = [at_("rstd%d" % i, [128, 1], F32) for i in range(2)]
    nmr = [at_("nmr%d" % i, [128, 1], F32) for i in range(2)]
    ln_idx = {}
    _nd0 = 0
    for _t in range(NTT):
        for _nm in ("mean", "vareps", "m2", "veps", "rr",
                    "nmr", "act0", "act"):
            _nd0 += 1
            ln_idx[(_t, _nm)] = _nd0
    assert _nd0 == LNOPS * NTT
    colsum = [at_("colsum%d" % i, [128, 256], BF16) for i in range(2)]
    bm = [at_("bm%d" % i, [4, 256], BF16, nbytes=512) for i in range(2)]
    emT = [at_("emT%d" % i, [128, 256], BF16) for i in range(2)]
    cbt = [at_("cbt%d" % i, [128, CW], BF16) for i in range(2)]
    bbf = [at_("bbf%d" % i, [128, CW], F32) for i in range(2)]

    arena = (off + 255) // 256 * 256
    wsum = [A("wsum%d" % i, [128, NKT * CW], BF16, offset=arena + i * 16384) for i in range(2)]
    wgt_db = [A("wgtdb%d" % i, [128, 8 * CW], BF16, offset=arena + 32768 + i * 4096) for i in range(2)]
    wgt_db += [A("wgtdb%d" % i, [128, 8 * CW], BF16, offset=arena + 49152 + (i - 2) * 4096)
               for i in (2, 3)]

    def wbuf(gi):
        return gi if gi < 4 else gi % 2
    _wfills = {0: 0, 1: 0, 2: 0, 3: 0}
    wgt_fill = {}
    for _gi in range(4 * NCH):
        _b = wbuf(_gi)
        _wfills[_b] += 1
        wgt_fill[_gi] = (_b, _wfills[_b])
    resid_db = [A("resid%d" % i, [128, 4 * CW], F32, offset=arena + 40960 + i * 4096) for i in range(2)]
    out_db = [A("outdb%d" % i, [128, 4 * CW], F32, offset=arena + 49152 + i * 4096) for i in range(2)]
    main_end = arena + 57344
    x_bufs = [A("x_buf%d" % i, [128, D], BF16, offset=arena + 16384 + i * 8192)
              for i in range(3)]

    def xb(t):
        return t % 3 if t < 14 else t - 14
    at_bf = A("at_bf", [128, NKT * 128], BF16, offset=arena + 40960)
    sq_scr = A("sq_scr", [128, D], BF16, offset=arena + 49152)
    assert main_end <= nc.sbuf_top, (main_end, nc.sbuf_top)

    ps_o = [nc.alloc_psum_tensor("ps_o%d" % i, [128, 512], F32) for i in range(2)]
    ps_x = [nc.alloc_psum_tensor("ps_x%d" % i, [128, 1024], BF16) for i in range(2)]
    ps_h = [nc.alloc_psum_tensor("ps_h%d" % i, [128, 512], F32) for i in range(2)]
    ps_m = nc.alloc_psum_tensor("ps_m", [128, 512], F32)
    ps_e = nc.alloc_psum_tensor("ps_e", [128, 512], F32)

    ctx = ExitStack()

    def sem(name):
        return ctx.enter_context(nc.semaphore(name))

    s_cst = sem("s_cst")
    s_x = [sem("s_x0"), sem("s_x1"), sem("s_x2")]
    s_msk = [sem("s_msk%d" % i) for i in range(4)]
    s_mskp = [sem("s_mskp%d" % i) for i in range(2)]
    s_mred2 = sem("s_mred2")
    s_rb = sem("s_rb")
    s_rsd = [sem("s_rsd0"), sem("s_rsd1")]
    s_at = sem("s_at")
    s_wgt = [sem("s_wgt%d" % i) for i in range(4)]
    s_wW = [[sem("s_wW%d_%d" % (i, g)) for g in range(4)] for i in range(2)]
    s_cb = [sem("s_cb0"), sem("s_cb1")]
    s_bb = [sem("s_bb0"), sem("s_bb1")]
    s_mred = sem("s_mred")
    s_pem = sem("s_pem")
    s_bm = sem("s_bm")
    s_emT = sem("s_emT")
    s_sq = sem("s_sq")
    s_id = sem("s_id")
    s_dvc = sem("s_dvc")
    s_r0 = sem("s_r0")

    s_peg = sem("s_peg")
    s_sev = sem("s_sev")
    s_peh = sem("s_peh")
    s_hrdy = sem("s_hrdy")
    s_cv = sem("s_cv")
    s_wm = sem("s_wm")
    s_add = sem("s_add")
    s_pmain = sem("s_pmain")
    s_ev = sem("s_ev")
    s_out = [sem("s_out0"), sem("s_out1")]
    s_dbg = sem("s_dbg") if debug else None

    def mask_dma(eng, c, qs=tuple(range(16))):
        for q in qs:
            m = c * 16 + q
            if m >= 4:
                eng.wait_ge(s_mred, m - 3)
            eng.dma_start(
                out=mask_db[m % 4][:].rearrange("p (a b) -> p a b", a=2, b=CW),
                in_=mask_h[:, c * CW:(c + 1) * CW]
                    .rearrange("(kt p) n -> p kt n", p=128)[:, q * 2:(q + 1) * 2, :],
            ).then_inc(s_msk[m % 4], 16)

    def mask_dma_main(eng, c, qs=tuple(range(8))):
        for q in qs:
            m8 = 8 * (c - 2) + q
            if m8 == 0:
                eng.wait_ge(s_mred, 30)
            elif m8 == 1:
                eng.wait_ge(s_mred, 32)
            else:
                eng.wait_ge(s_mred2, m8 - 1)
            eng.dma_start(
                out=mask_dbp[m8 % 2][:].rearrange("p (a b) -> p a b", a=4, b=CW),
                in_=mask_h[:, c * CW:(c + 1) * CW]
                    .rearrange("(kt p) n -> p kt n", p=128)[:, q * 4:(q + 1) * 4, :],
            ).then_inc(s_mskp[m8 % 2], 16)

    def mask_reduce(vector, c, q):
        m = c * 16 + q
        par = c % 2
        vector.wait_ge(s_msk[m % 4], 16 * (m // 4 + 1))
        with nc.allow_low_precision(reason="int mask sums <=32 exact in bf16"):
            vector.tensor_reduce(
                out=colsum[par][:, q * 16:(q + 1) * 16].rearrange("p (a b) -> p a b", a=2, b=8),
                in_=mask_db[m % 4][:].rearrange("p (a b c) -> p a b c", a=2, b=8, c=32),
                axis=AX.X, op=ALU.add,
            ).then_inc(s_mred, 1)

    def mask_reduce_main(vector, c, q):
        m8 = 8 * (c - 2) + q
        par = c % 2
        vector.wait_ge(s_mskp[m8 % 2], 16 * (m8 // 2 + 1))
        if q == 0:
            vector.wait_ge(s_pem, 2 * (c - 2) + 1)
        with nc.allow_low_precision(reason="int mask sums <=32 exact in bf16"):
            vector.tensor_reduce(
                out=colsum[par][:, q * 32:(q + 1) * 32].rearrange("p (a b) -> p a b", a=4, b=8),
                in_=mask_dbp[m8 % 2][:].rearrange("p (a b c) -> p a b c", a=4, b=8, c=32),
                axis=AX.X, op=ALU.add,
            ).then_inc(s_mred2, 1)

    def mask_pe(tensor, c):
        par = c % 2
        if c <= 1:
            tensor.wait_ge(s_mred, 16 * (c + 1))
        else:
            tensor.wait_ge(s_mred2, 8 * (c - 1))
        if c >= 2:
            tensor.wait_ge(s_bm, c - 1)
        tensor.matmul(ps_m[0:4, 0:256], g32[:], colsum[par][:], start=True, stop=True).then_inc(s_pem, 1)

    def mask_pe2(tensor, c):
        par = c % 2
        tensor.wait_ge(s_bm, c + 1)
        if c >= 2:
            tensor.wait_ge(s_emT, c - 1)
        tensor.matmul(ps_e[:, 0:256], rep4[:], bm[par][:], start=True, stop=True).then_inc(s_pem, 1)

    def mask_min(vector, c):
        par = c % 2
        vector.wait_ge(s_pem, 2 * c + 1)
        with nc.allow_low_precision(reason="0/1 threshold exact in bf16"):
            vector.tensor_scalar_min(out=bm[par][:], in0=ps_m[0:4, 0:256], scalar1=1.0).then_inc(s_bm, 1)

    def mask_evac(vector, c):
        par = c % 2
        vector.wait_ge(s_pem, 2 * c + 2)
        with nc.allow_low_precision(reason="0/1 mask exact in bf16"):
            vector.tensor_copy(out=emT[par][:], in_=ps_e[:, 0:256]).then_inc(s_emT, 1)

    def wm_op(vector, c, g):
        gi = 4 * c + g
        _b, _k = wgt_fill[gi]
        vector.wait_ge(s_wgt[_b], 16 * _k)
        vector.wait_ge(s_emT, c + 1)
        wg = wgt_db[wbuf(gi)][:].rearrange("p (a b c) -> p a b c", a=8, b=8, c=32)
        emb = emT[c % 2][:, g * 64:(g + 1) * 64] \
            .rearrange("p (a b) -> p a b", a=8, b=8).unsqueeze(3).broadcast_to((128, 8, 8, 32))
        vector.tensor_tensor(out=wg, in0=wg, in1=emb, op=ALU.mult).then_inc(s_wm, 1)

    def add_op(vector, c, g):
        pc = c % 2
        vector.wait_ge(s_wm, c * 4 + g + 1)
        vector.wait_ge(s_wW[pc][g], 16 * (c // 2 + 1))
        vector.tensor_tensor(
            out=wsum[pc][:, g * 8 * CW:(g + 1) * 8 * CW],
            in0=wsum[pc][:, g * 8 * CW:(g + 1) * 8 * CW],
            in1=wgt_db[wbuf(4 * c + g)][:],
            op=ALU.add,
        ).then_inc(s_add, 1)

    with nc.Block() as block:

        @block.sync
        def _(sync):
            sync.dma_start(out=i128[:], in_=id_h).then_inc(s_cst, 16)
            sync.dma_start(out=g32[:], in_=g32_h).then_inc(s_cst, 16)
            sync.dma_start(out=rep4[:], in_=rep4_h).then_inc(s_cst, 16)
            sync.dma_start(out=ones_sb[:], in_=ones_h).then_inc(s_cst, 16)
            sync.dma_start(out=lw_sb[:], in_=lw_h).then_inc(s_cst, 16)
            mask_dma(sync, 0)
            mask_dma(sync, 1)
            mask_dma_main(sync, 2)
            for c in range(NCH):
                if c + 3 < NCH:
                    mask_dma_main(sync, c + 3)

        @block.gpsimd
        def _(gpsimd):
            def wgt_dma(c, g):
                gi = 4 * c + g
                if c == 0 and g == 0:
                    gpsimd.wait_ge(s_dvc, ln_idx[(11, "act")])
                    gpsimd.wait_ge(s_sq, 12)
                    gpsimd.wait_ge(s_peg, 4 * 12)
                elif c == 0 and g == 2:
                    gpsimd.wait_ge(s_sq, NTT)
                if gi >= 4:
                    prev = gi % 2 if gi < 6 else gi - 2
                    gpsimd.wait_ge(s_add, prev + 1)
                gpsimd.dma_start(
                    out=wgt_db[wbuf(gi)][:].rearrange("p (a b) -> p a b", a=8, b=CW),
                    in_=wgt_h[:, c * CW:(c + 1) * CW]
                        .rearrange("(kt p) n -> p kt n", p=128)[:, g * 8:(g + 1) * 8, :],
                ).then_inc(s_wgt[wbuf(gi)], 16)

            def w_dma(c, g):
                pc = c % 2
                if c == 1:
                    gpsimd.wait_ge(s_dvc, ln_idx[(NTT - 1, "act")])
                    gpsimd.wait_ge(s_peg, 4 * NTT)
                elif c >= 2:
                    gpsimd.wait_ge(s_pmain, (c - 1) * NTT)
                gpsimd.dma_start(
                    out=wsum[pc][:, g * 8 * CW:(g + 1) * 8 * CW]
                        .rearrange("p (a b) -> p a b", a=8, b=CW),
                    in_=wt_h[:, c * CW:(c + 1) * CW]
                        .rearrange("(kt p) n -> p kt n", p=128)[:, g * 8:(g + 1) * 8, :],
                ).then_inc(s_wW[pc][g], 16)

            for t in range(NTT):
                tp = t - 3 if t < 14 else t - 2
                if t >= 3:
                    gpsimd.wait_ge(s_dvc, ln_idx[(tp, "act")])
                    gpsimd.wait_ge(s_sq, tp + 1)
                    gpsimd.wait_ge(s_peg, 4 * tp + 2)
                gpsimd.dma_start(out=x_bufs[xb(t)][:, 0:2048],
                                 in_=x_h[t * 128:(t + 1) * 128, 0:2048]).then_inc(s_x[xb(t)], 16)
                if t >= 3:
                    gpsimd.wait_ge(s_peg, 4 * tp + 4)
                gpsimd.dma_start(out=x_bufs[xb(t)][:, 2048:4096],
                                 in_=x_h[t * 128:(t + 1) * 128, 2048:4096]).then_inc(s_x[xb(t)], 16)
                if t == 0:
                    for i in range(2):
                        gpsimd.dma_start(
                            out=at_bf[:, i * 2048:(i + 1) * 2048].rearrange("p (a b) -> p a b", a=16, b=128),
                            in_=at_h.rearrange("(kt p) a -> p kt a", p=128)[:, i * 16:(i + 1) * 16, :],
                        ).then_inc(s_at, 16)
                elif 2 <= t <= 5:
                    w_dma(0, t - 2)
                elif t == 6:
                    gpsimd.dma_start(out=cbt[0][:], in_=bt_h[:, 0:CW]).then_inc(s_cb[0], 16)
                    gpsimd.dma_start(out=bbf[0][:],
                                     in_=bias_h[0:CW].unsqueeze(0).broadcast_to((128, CW))
                                     ).then_inc(s_bb[0], 16)
            for g in range(4):
                wgt_dma(0, g)

            for c in range(1, NCH):
                pc = c % 2
                wgt_dma(c, 0)
                wgt_dma(c, 1)
                for g in range(4):
                    w_dma(c, g)
                if c >= 2:
                    gpsimd.wait_ge(s_pmain, (c - 1) * NTT)
                gpsimd.dma_start(out=cbt[pc][:], in_=bt_h[:, c * CW:(c + 1) * CW]).then_inc(s_cb[pc], 16)
                gpsimd.dma_start(out=bbf[pc][:],
                                 in_=bias_h[c * CW:(c + 1) * CW].unsqueeze(0).broadcast_to((128, CW))
                                 ).then_inc(s_bb[pc], 16)
                wgt_dma(c, 2)
                wgt_dma(c, 3)
                if debug and c == 1:
                    gpsimd.wait_ge(s_sev, 4 * NTT)
                    gpsimd.dma_start(out=dbg_xnT, in_=xnT[:, 0:2048]).then_inc(s_dbg, 16)
                    gpsimd.wait_ge(s_emT, 1)
                    gpsimd.dma_start(out=dbg_emT, in_=emT[0][:]).then_inc(s_dbg, 16)
                    gpsimd.wait_ge(s_hrdy, 4)
                    gpsimd.dma_start(out=dbg_h, in_=h_sb[:]).then_inc(s_dbg, 16)
                    gpsimd.wait_ge(s_add, 4)
                    gpsimd.dma_start(out=dbg_ws, in_=wsum[0][:]).then_inc(s_dbg, 16)
                    gpsimd.wait_ge(s_cv, 1)
                    gpsimd.dma_start(out=dbg_cv, in_=cvec[:]).then_inc(s_dbg, 16)
                    gpsimd.wait_ge(s_dbg, 80)

        @block.vector
        def _(vector):
            nd = 0

            vector.wait_ge(s_cst, 5 * 16)
            vector.tensor_scalar_mul(out=cvec[:], in0=lw_sb[:], scalar1=2.0).then_inc(s_cv, 1)

            def dv(ins, tile=None, name=None):
                nonlocal nd
                if nd:
                    vector.wait_ge(s_dvc, nd)
                r = ins()
                nd += 1
                r.then_inc(s_dvc, 1)
                if name is not None:
                    assert ln_idx[(tile, name)] == nd, (tile, name, nd)

            def chain_a(t):
                p = t % 2
                vector.wait_ge(s_id, t + 1)
                dv(lambda: vector.tensor_scalar_mul(out=mean[p][:], in0=s1[p][:], scalar1=1.0 / D),
                   t, "mean")
                vector.wait_ge(s_sq, t + 1)
                dv(lambda: vector.tensor_scalar(out=vareps[p][:], in0=s2[p][:], scalar1=1.0 / D,
                                                scalar2=EPS, op0=ALU.mult, op1=ALU.add),
                   t, "vareps")
                dv(lambda: vector.tensor_mul(out=m2[p][:], in0=mean[p][:], in1=mean[p][:]))
                dv(lambda: vector.tensor_sub(out=veps[p][:], in0=vareps[p][:], in1=m2[p][:]))
                dv(lambda: vector.reciprocal(out=rr[p][:], in_=veps[p][:]), t, "rr")

            def chain_b(t):
                p = t % 2
                vector.wait_ge(s_r0, t + 1)
                dv(lambda: vector.scalar_tensor_tensor(out=nmr[p][:], in0=mean[p][:], scalar=-1.0,
                                                       in1=r0[p][:], op0=ALU.mult, op1=ALU.mult),
                   t, "nmr")
                with nc.allow_low_precision(reason="xn is a bf16 matmul operand"):
                    dv(lambda: vector.tensor_scalar(out=x_bufs[xb(t)][:, 0:2048],
                                                    in0=x_bufs[xb(t)][:, 0:2048],
                                                    scalar1=r0[p][:], scalar2=nmr[p][:],
                                                    op0=ALU.mult, op1=ALU.add), t, "act0")
                    dv(lambda: vector.tensor_scalar(out=x_bufs[xb(t)][:, 2048:4096],
                                                    in0=x_bufs[xb(t)][:, 2048:4096],
                                                    scalar1=r0[p][:], scalar2=nmr[p][:],
                                                    op0=ALU.mult, op1=ALU.add), t, "act")

            for t in range(NTT + 1):
                if t >= 1:
                    for sg in range(4):
                        g = 4 * (t - 1) + sg
                        vector.wait_ge(s_peg, g + 1)
                        vector.tensor_copy(
                            out=xnT[:].rearrange("p (kt tt) -> p kt tt", kt=NKT)
                                [:, 8 * sg:8 * sg + 8, (t - 1) * 128:t * 128],
                            in_=ps_x[g % 2][:].rearrange("p (a b) -> p a b", a=8, b=128),
                        ).then_inc(s_sev, 1)
                if t < NTT:
                    chain_a(t)
                    chain_b(t)
                if t <= 15:
                    mask_reduce(vector, (2 * t) // 16, (2 * t) % 16)
                    mask_reduce(vector, (2 * t + 1) // 16, (2 * t + 1) % 16)
                if t == 9:
                    mask_min(vector, 0)
                elif t == 10:
                    mask_evac(vector, 0)
                elif t == 15:
                    wm_op(vector, 0, 0)
                elif t == 16:
                    mask_min(vector, 1)
                    mask_evac(vector, 1)
                    add_op(vector, 0, 0)
                    wm_op(vector, 0, 1)
            assert nd == LNOPS * NTT, nd
            add_op(vector, 0, 1)
            for g in range(2, 4):
                wm_op(vector, 0, g)
                add_op(vector, 0, g)

            for c in range(NCH):
                for tt in range(NTT):
                    mm = c * NTT + tt
                    og = c * 4 + tt // 4
                    if c == 0:
                        if tt < 8:
                            mask_reduce_main(vector, 2, tt)
                        elif tt == 9:
                            mask_min(vector, 2)
                        elif tt == 10:
                            mask_evac(vector, 2)
                    elif c + 2 < NCH:
                        if tt == 0:
                            mask_min(vector, c + 2)
                        elif tt == 1:
                            mask_evac(vector, c + 2)
                    vector.wait_ge(s_pmain, mm + 1)
                    if og < 2:
                        vector.wait_ge(s_sq, NTT)
                    vector.wait_ge(s_rsd[og % 2], 16 * (og // 2 + 1))
                    if og >= 2 and tt % 4 == 0:
                        vector.wait_ge(s_out[og % 2], 16 * (og // 2))
                    if tt % 4 == 0:
                        vector.wait_ge(s_bb[c % 2], 16 * (c // 2 + 1))
                        vector.tensor_tensor(
                            out=resid_db[og % 2][:].rearrange("p (j n) -> p j n", j=4),
                            in0=resid_db[og % 2][:].rearrange("p (j n) -> p j n", j=4),
                            in1=bbf[c % 2][:].unsqueeze(1).broadcast_to((128, 4, CW)),
                            op=ALU.add,
                        ).then_inc(s_rb, 1)
                        vector.wait_ge(s_rb, og + 1)
                    vector.scalar_tensor_tensor(
                        out=out_db[og % 2][:, (tt % 4) * CW:(tt % 4 + 1) * CW],
                        in0=ps_o[mm % 2][:, 0:CW], scalar=1.0,
                        in1=resid_db[og % 2][:, (tt % 4) * CW:(tt % 4 + 1) * CW],
                        op0=ALU.mult, op1=ALU.add,
                    ).then_inc(s_ev, 1)
                    if c + 1 < NCH and 2 <= tt < 10:
                        g = (tt - 2) // 2
                        if tt % 2 == 0:
                            wm_op(vector, c + 1, g)
                        else:
                            add_op(vector, c + 1, g)
                    if c + 3 < NCH and 7 <= tt < 15:
                        mask_reduce_main(vector, c + 3, tt - 7)

        @block.tensor
        def _(tensor):
            tensor.wait_ge(s_cst, 5 * 16)
            for t in range(NTT):
                tensor.wait_ge(s_dvc, ln_idx[(t, "act0")])
                for sg in range(4):
                    g = 4 * t + sg
                    if sg == 2:
                        tensor.wait_ge(s_dvc, ln_idx[(t, "act")])
                    if g >= 2:
                        tensor.wait_ge(s_sev, g - 1)
                    for j in range(8):
                        ins = tensor.transpose(ps_x[g % 2][:, j * 128:(j + 1) * 128],
                                               x_bufs[xb(t)][:, (8 * sg + j) * 128:(8 * sg + j + 1) * 128],
                                               i128[:])
                        if j == 7:
                            ins.then_inc(s_peg, 1)
                if t == 15:
                    mask_pe(tensor, 1)
                if t % 4 == 3:
                    tc = t // 4
                    tensor.wait_ge(s_sev, 4 * (t + 1))
                    tensor.wait_ge(s_at, 32)
                    if tc >= 2:
                        tensor.wait_ge(s_hrdy, tc - 1)
                    for kt in range(NKT):
                        ins = tensor.matmul(ps_h[tc % 2][:, 0:512],
                                            at_bf[:, kt * 128:(kt + 1) * 128],
                                            xnT[:, kt * T + tc * 512:kt * T + tc * 512 + 512],
                                            start=(kt == 0), stop=(kt == NKT - 1))
                        if kt == NKT - 1:
                            ins.then_inc(s_peh, 1)
                if t == 8:
                    mask_pe(tensor, 0)
                elif t == 9:
                    mask_pe2(tensor, 0)
                elif t == 15:
                    mask_pe2(tensor, 1)
            tensor.wait_ge(s_hrdy, 4)
            for c in range(NCH):
                pc = c % 2
                tensor.wait_ge(s_add, (c + 1) * 4)
                tensor.wait_ge(s_cb[pc], 16 * (c // 2 + 1))
                if c >= 1 and c + 2 < NCH:
                    mask_pe(tensor, c + 2)
                for tt in range(NTT):
                    mm = c * NTT + tt
                    if mm >= 2:
                        tensor.wait_ge(s_ev, mm - 1)
                    for kt in range(NKT):
                        tensor.matmul(ps_o[mm % 2][:, 0:CW],
                                      xnT[:, kt * T + tt * 128:kt * T + tt * 128 + 128],
                                      wsum[pc][:, kt * CW:(kt + 1) * CW],
                                      start=(kt == 0), stop=False)
                    tensor.matmul(ps_o[mm % 2][:, 0:CW],
                                  h_sb[:, tt * 128:(tt + 1) * 128],
                                  cbt[pc][:], start=False, stop=True).then_inc(s_pmain, 1)
                    if c == 0:
                        if tt == 8:
                            mask_pe(tensor, 2)
                        elif tt == 9:
                            mask_pe2(tensor, 2)
                    elif c + 2 < NCH and tt == 0:
                        mask_pe2(tensor, c + 2)

        @block.scalar
        def _(scalar):
            for t in range(NTT):
                nfill = t // 3 + 1 if t < 14 else 6
                scalar.wait_ge(s_x[xb(t)], 32 * nfill)
                if t >= 1:
                    scalar.wait_ge(s_sq, t)
                if t >= 2:
                    scalar.wait_ge(s_dvc, ln_idx[(t - 2, "mean")])
                scalar.activation(out=sq_scr[:], in_=x_bufs[xb(t)][:], func=AF.Identity,
                                  accum_out=s1[t % 2][:]).then_inc(s_id, 1)
                scalar.wait_ge(s_id, t + 1)
                if t >= 2:
                    scalar.wait_ge(s_dvc, ln_idx[(t - 2, "vareps")])
                scalar.activation(out=sq_scr[:], in_=x_bufs[xb(t)][:], func=AF.Square,
                                  accum_out=s2[t % 2][:]).then_inc(s_sq, 1)
                scalar.wait_ge(s_dvc, ln_idx[(t, "rr")])
                if t >= 2:
                    scalar.wait_ge(s_dvc, ln_idx[(t - 2, "act")])
                scalar.activation(out=r0[t % 2][:], in_=rr[t % 2][:], func=AF.Sqrt).then_inc(s_r0, 1)
                if t % 4 == 1 and t >= 5:
                    tc = (t - 5) // 4
                    if tc == 0:
                        scalar.wait_ge(s_cv, 1)
                    scalar.wait_ge(s_peh, tc + 1)
                    scalar.activation(out=h_sb[:, tc * 512:(tc + 1) * 512], in_=ps_h[tc % 2][:, 0:512],
                                      func=AF.Identity, scale=cvec[:]).then_inc(s_hrdy, 1)
            scalar.wait_ge(s_peh, 4)
            scalar.activation(out=h_sb[:, 3 * 512:4 * 512], in_=ps_h[1][:, 0:512],
                              func=AF.Identity, scale=cvec[:]).then_inc(s_hrdy, 1)
            def resid_dma(og):
                c2, q2 = og // 4, og % 4
                if og < 2:
                    scalar.wait_ge(s_peh, 4)
                scalar.dma_start(
                    out=resid_db[og % 2][:].rearrange("p (j n) -> p j n", j=4),
                    in_=x_h[q2 * 512:(q2 + 1) * 512, c2 * CW:(c2 + 1) * CW]
                        .rearrange("(j p) n -> p j n", p=128),
                ).then_inc(s_rsd[og % 2], 16)

            resid_dma(0)
            resid_dma(1)
            for og in range(4 * NCH):
                c, q = og // 4, og % 4
                scalar.wait_ge(s_ev, (og + 1) * 4)
                if og + 2 < 4 * NCH:
                    resid_dma(og + 2)
                scalar.dma_start(
                    out=out_h[q * 512:(q + 1) * 512, c * CW:(c + 1) * CW]
                        .rearrange("(j p) n -> p j n", p=128),
                    in_=out_db[og % 2][:].rearrange("p (j n) -> p j n", j=4),
                ).then_inc(s_out[og % 2], 16)
            scalar.wait_ge(s_out[0], 16 * 2 * NCH)
            scalar.wait_ge(s_out[1], 16 * 2 * NCH)

    return nc


def _consts():
    ident = np.eye(128, dtype=ml_dtypes.bfloat16)
    g32 = np.zeros((128, 4), dtype=ml_dtypes.bfloat16)
    rep4 = np.zeros((4, 128), dtype=ml_dtypes.bfloat16)
    for n in range(128):
        g32[n, n // 32] = 1
        rep4[n // 32, n] = 1
    ones_row = np.ones((1, 128), dtype=ml_dtypes.bfloat16)
    return ident, g32, rep4, ones_row


def kernel(L0_attn_data, L0_attn_weight, L0_attn_mask, L0_ffn_lora_A, L0_ffn_lora_B,
           L0_ffn_W, L0_ffn_bias, L0_ffn_lora_weights):
    x = np.asarray(L0_attn_data, dtype=np.float32)
    wgtT = np.ascontiguousarray(np.asarray(L0_attn_weight, dtype=np.float32).T)
    maskT = np.ascontiguousarray(np.asarray(L0_attn_mask, dtype=np.int32).T)
    WT = np.ascontiguousarray(np.asarray(L0_ffn_W, dtype=np.float32).T)
    A_T = np.ascontiguousarray(np.asarray(L0_ffn_lora_A, dtype=np.float32).reshape(128, D).T)
    B_t = np.ascontiguousarray(np.asarray(L0_ffn_lora_B, dtype=np.float32).transpose(0, 2, 1).reshape(128, D))
    bias = np.ascontiguousarray(np.asarray(L0_ffn_bias, dtype=np.float32))
    lw_rep = np.ascontiguousarray(
        np.repeat(np.asarray(L0_ffn_lora_weights, dtype=np.float32), 16).reshape(128, 1))
    ident, g32, rep4, ones_row = _consts()

    nc = build_nc()
    in_maps = []
    for c in range(8):
        xs = np.ascontiguousarray(x[:, 1024 * c:1024 * (c + 1), :].reshape(T, D))
        in_maps.append(dict(x=xs, wt=WT, wgt=wgtT, maskt=maskT, at=A_T, bt=B_t,
                            bias=bias, lwrep=lw_rep, ident=ident, g32=g32,
                            rep4=rep4, onesrow=ones_row))
    res = run_bass_kernel_spmd(nc, in_maps, core_ids=list(range(8)))
    out = np.empty((2, 8192, D), dtype=np.float32)
    for c in range(8):
        out[:, 1024 * c:1024 * (c + 1), :] = np.asarray(res.results[c]["out"]).reshape(2, 1024, D)
    return out
